# version 25
# speedup vs baseline: 1.0196x; 1.0029x over previous
import contextlib
import numpy as np

import concourse.bass as bass
import concourse.mybir as mybir
from concourse.bass_utils import run_bass_kernel_spmd

F32 = mybir.dt.float32
BF16 = mybir.dt.bfloat16
AF = mybir.ActivationFunctionType
ALU = mybir.AluOpType
AX = mybir.AxisListType

B, S, D = 2, 8192, 2048
H, HD = 16, 128
M = 4
CHUNK = 128
NCORES = 8
TPC = 2048
NCH = TPC // CHUNK
DC = D // 128
NS = 4
SW = 512
SCALE = HD ** -0.5
EPS = 1e-6
LNEPS = 1e-5
TOPK = 4
NA = 64
FFN_STRIPS = None
NEUTER = set()
FFN_SUB = 7
NR = 16
H2S = 3



def _tile_wT(w):
    return np.ascontiguousarray(
        w.reshape(16, 128, 16, 128).transpose(0, 3, 2, 1).reshape(16, 128, 2048))


def prep_inputs(inputs):
    x = inputs["L0_attn_hidden_states"]
    mem = inputs["L0_attn_memories"]
    mnorm = inputs["L0_attn_memory_norms"]
    gate = inputs["L0_attn_gate"]
    A = inputs["L0_ffn_A"]
    Bm = inputs["L0_ffn_B"]
    bias = inputs["L0_ffn_bias"]

    shared = {}
    shared["wq_l"] = _tile_wT(inputs["L0_attn_w_q"])
    shared["wk_l"] = _tile_wT(inputs["L0_attn_w_k"])
    shared["wv_l"] = _tile_wT(inputs["L0_attn_w_v"])
    shared["wo_l"] = _tile_wT(inputs["L0_attn_w_o"])
    shared["wf_l"] = _tile_wT(inputs["L0_ffn_W"])
    shared["mem_l"] = np.ascontiguousarray(
        mem.transpose(2, 1, 0, 3).reshape(128, H * M * 128))
    shared["mnormfm_l"] = np.ascontiguousarray(mnorm.transpose(1, 2, 0))
    blk = np.zeros((H, 128, 64), np.float32)
    for h in range(H):
        for m in range(M):
            blk[h, :, 4 * h + m] = mnorm[m, h, :]
    shared["mnormblk_l"] = blk
    shared["gate_l"] = np.ascontiguousarray(gate.reshape(1, H))
    A_flat = A.reshape(NA * NR, D)
    shared["a_l"] = np.ascontiguousarray(
        A_flat.reshape(8, 128, 16, 128).transpose(0, 3, 2, 1).reshape(8, 128, 2048))
    B2 = Bm.transpose(0, 2, 1).reshape(NA * NR, D)
    shared["b2_l"] = np.ascontiguousarray(B2.reshape(8, 128, 2048))
    shared["bias_l"] = np.ascontiguousarray(bias.reshape(16, 128).T)

    import ml_dtypes
    BF = ml_dtypes.bfloat16
    shared["ident_bf"] = np.eye(128, dtype=BF)
    shared["ones128_f"] = np.ones((33, 128), np.float32)
    shared["onescol_bf"] = np.ones((128, 1), dtype=BF)
    triu = np.triu(np.ones((128, 128), np.float32))
    shared["mask4_f"] = np.ascontiguousarray(np.concatenate([triu] * 4, axis=1))
    sel = np.zeros((80, 80, 128), BF)
    for i in range(80):
        sel[i, i, :] = 1
    shared["sel80_bf"] = np.ascontiguousarray(sel.reshape(80, 80 * 128))
    G = np.zeros((8, 128, 64), np.float32)
    E = np.zeros((8, 64, 128), BF)
    for ac in range(8):
        for p in range(128):
            G[ac, p, ac * 8 + p // 16] = 1.0
            E[ac, ac * 8 + p // 16, p] = 1
    shared["g_l"] = G
    shared["e_l"] = np.ascontiguousarray(E)
    Hs = np.zeros((128, 16), BF)
    for p in range(128):
        Hs[p, p % 16] = 1
    shared["hsum_bf"] = Hs
    shared["hbc_bf"] = np.ascontiguousarray(Hs.T)
    shared["neginf_f"] = np.full((128, 64), -3.0e38, np.float32)
    shared["eps_l"] = np.full((1, 2), 1e-5, np.float32)

    in_maps = []
    for c in range(NCORES):
        b, q4 = c // 4, c % 4
        xc = np.ascontiguousarray(x[b, q4 * TPC:(q4 + 1) * TPC, :].T)
        mcc = np.zeros((128, 16), np.float32)
        for r in range(b * 4, c):
            mcc[:, r] = 1.0
        for r in range(b * 4, b * 4 + 4):
            mcc[:, 8 + r] = 1.0
        m = dict(shared)
        m["x"] = xc
        m["maskcc"] = mcc
        in_maps.append(m)
    return in_maps


class Ctr:

    def __init__(self, handle):
        self.h = handle
        self.n = 0

    def inc(self, inst, k=1):
        inst.then_inc(self.h, k)
        self.n += k
        return self.n


def build(dbg=False, upto="all"):
    nc = bass.Bass("TRN2", target_bir_lowering=False)
    stages = ["A", "B", "ln1", "ffn", "all"]
    want = stages.index(upto)

    def ap(name, shape, dtype, kind):
        return nc.dram_tensor(name, shape, dtype, kind=kind).ap()

    EI = "ExternalInput"
    x_h = ap("x", [D, TPC], F32, EI)
    wq_h = ap("wq_l", [H, 128, 2048], F32, EI)
    wk_h = ap("wk_l", [H, 128, 2048], F32, EI)
    wv_h = ap("wv_l", [H, 128, 2048], F32, EI)
    wo_h = ap("wo_l", [16, 128, 2048], F32, EI)
    wf_h = ap("wf_l", [16, 128, 2048], F32, EI)
    mem_h = ap("mem_l", [128, H * M * 128], F32, EI)
    mnormfm_h = ap("mnormfm_l", [H, 128, M], F32, EI)
    mnormblk_h = ap("mnormblk_l", [H, 128, 64], F32, EI)
    gate_h = ap("gate_l", [1, H], F32, EI)
    a_h = ap("a_l", [8, 128, 2048], F32, EI)
    b2_h = ap("b2_l", [8, 128, 2048], F32, EI)
    bias_h = ap("bias_l", [128, 16], F32, EI)
    ident_h = ap("ident_bf", [128, 128], BF16, EI)
    ones128_h = ap("ones128_f", [33, 128], F32, EI)
    onescol_h = ap("onescol_bf", [128, 1], BF16, EI)
    mask4_h = ap("mask4_f", [128, 512], F32, EI)
    sel80_h = ap("sel80_bf", [80, 80 * 128], BF16, EI)
    g_h = ap("g_l", [8, 128, 64], F32, EI)
    e_h = ap("e_l", [8, 64, 128], BF16, EI)
    hsum_h = ap("hsum_bf", [128, 16], BF16, EI)
    hbc_h = ap("hbc_bf", [16, 128], BF16, EI)
    neginf_h = ap("neginf_f", [128, 64], F32, EI)
    eps_h = ap("eps_l", [1, 2], F32, EI)
    maskcc_h = ap("maskcc", [128, 16], F32, EI)

    out_h = ap("out", [D, TPC], F32, "ExternalOutput")

    spk = "ExternalOutput" if dbg else "Internal"
    sp_num = ap("sp_num", [H, 128, TPC], BF16, spk)
    sp_den = ap("sp_den", [H, 1, TPC], BF16, spk)
    sp_attn = ap("sp_attn", [H, 128, TPC], BF16, spk)
    sp_x1 = ap("sp_x1", [16, 128, TPC], F32, spk)
    sp_x1b = ap("sp_x1b", [16, 128, TPC], BF16, spk)
    cc_in_a = nc.dram_tensor("cc_in_a", [128, (H - H2S) * 129 + (H - H2S)], F32,
                             kind="Internal")
    cc_out_a = nc.dram_tensor("cc_out_a", [NCORES * 128, (H - H2S) * 129 + (H - H2S)],
                              F32, kind="Internal")
    cc_in_b = nc.dram_tensor("cc_in_b", [128, H2S * 129 + H2S], F32,
                             kind="Internal")
    cc_out_b = nc.dram_tensor("cc_out_b", [NCORES * 128, H2S * 129 + H2S], F32,
                              kind="Internal")
    if dbg:
        dbg_qp = ap("dbg_qp", [H, 128, TPC], BF16, "ExternalOutput")
        dbg_kvtot = ap("dbg_kvtot", [128, H * 129], F32, "ExternalOutput")
        dbg_qsum = ap("dbg_qsum", [128, H], F32, "ExternalOutput")
        dbg_soff = ap("dbg_soff", [128, H * 129], F32, "ExternalOutput")
        dbg_cb = ap("dbg_cb", [80, TPC], F32, "ExternalOutput")
        dbg_wg = ap("dbg_wg", [1, H * M], F32, "ExternalOutput")

    es = contextlib.ExitStack()
    with es:
        sb = lambda name, shape, dt: es.enter_context(nc.sbuf_tensor(name, shape, dt))
        mksem = lambda name: Ctr(es.enter_context(nc.semaphore(name)))

        xsum = sb("xsum", [128, DC], F32)
        xsum_bf = sb("xsum_bf", [128, DC], BF16)
        qsum_all = sb("qsum_all", [128, H], F32)
        ident = sb("ident", [128, 128], BF16)
        onesc = sb("onesc", [128, 1], BF16)
        ones128 = sb("ones128", [33, 128], F32)
        mask4 = sb("mask4", [128, 512], F32)
        gate_sb = sb("gate_sb", [1, H], F32)
        maskcc = sb("maskcc_sb", [128, 16], F32)
        bias_sb = sb("bias_sb", [128, 16], F32)
        eps_sb = sb("eps_sb", [1, 2], F32)

        esAB = es.enter_context(contextlib.ExitStack())
        kvtot = esAB.enter_context(nc.sbuf_tensor("kvtot", [128, H * 129], F32))
        qp_all = esAB.enter_context(nc.sbuf_tensor("qp_all", [128, H * TPC], BF16))
        esAx = es.enter_context(contextlib.ExitStack())
        x_bf = esAx.enter_context(nc.sbuf_tensor("x_bf", [128, DC * TPC], BF16))

        s_xq = [mksem(f"s_xq{q}") for q in range(4)]
        s_wq = [mksem("s_wq0"), mksem("s_wq1")]
        s_wk = mksem("s_wk")
        s_wv = mksem("s_wv")
        s_cld = mksem("s_cld")
        s_pe = mksem("s_pe")
        s_dve = mksem("s_dve")
        s_act = mksem("s_act")
        s_spn = [mksem("s_spn0"), mksem("s_spn1")]
        s_spd = [mksem("s_spd0"), mksem("s_spd1")]
        s_cc = mksem("s_cc")
        s_ccb = mksem("s_ccb")
        s_bld = mksem("s_bld")
        s_gld = mksem("s_gld")
        s_pool = mksem("s_pool")
        s_out = mksem("s_out")

        blk = es.enter_context(nc.Block())

        @blk.sync
        def _(sync):
            sync.dma_start(out=ident[:, :], in_=ident_h[:, :]).then_inc(s_cld.h, 16)
            sync.dma_start(out=onesc[:, :], in_=onescol_h[:, :]).then_inc(s_cld.h, 16)
            sync.dma_start(out=ones128[:, :], in_=ones128_h[:, :]).then_inc(s_cld.h, 16)
            sync.dma_start(out=mask4[:, :], in_=mask4_h[:, :]).then_inc(s_cld.h, 16)
            sync.dma_start(out=gate_sb[:, :], in_=gate_h[:, :]).then_inc(s_cld.h, 16)
            sync.dma_start(out=maskcc[:, :], in_=maskcc_h[:, :]).then_inc(s_cld.h, 16)
            sync.dma_start(out=eps_sb[:, :], in_=eps_h[:, :]).then_inc(s_cld.h, 16)
        s_cld.n = 7 * 16
        CLD = s_cld.n

        H2 = H2S
        A_PE = ([("q", s) for s in range(NS)] + [("qsum", 0)]
                + [("k", s) for s in range(NS)] + [("v", s) for s in range(NS)]
                + [("kpT", g) for g in range(4)] + [("vT", g) for g in range(4)])
        for g in range(4):
            A_PE += [("kv", 4 * g + n) for n in range(4)]
            A_PE += [("scT", g)]
            if g >= 1:
                A_PE += [("num", g - 1), ("den", g - 1)]
        A_PE += [("num", 3), ("den", 3)]
        A_NPE = len(A_PE)
        A_IPE = {k: i for i, k in enumerate(A_PE)}

        def peA(h, key):
            return h * A_NPE + A_IPE[key] + 1

        A_DV = [("ms", 0)]
        for s in range(NS):
            A_DV += [("qmin", s), ("qstt", s)]
        for s in range(NS):
            A_DV += [("kmin", s), ("kstt", s)]
        for g in range(4):
            for n in range(4):
                A_DV += [("kvcp", 4 * g + n), ("kvadd", 4 * g + n)]
            A_DV += [("scm", g)]
        A_DV += [("kvtot", 0)]
        A_NDV = len(A_DV)
        A_IDV = {k: i for i, k in enumerate(A_DV)}
        DVE0 = 17

        def dvA(h, key):
            return DVE0 + h * A_NDV + A_IDV[key] + 1

        A_AC = ([("qexp", s) for s in range(NS)] + [("kexp", s) for s in range(NS)]
                + [("vcp", s) for s in range(NS)] + [("qsum_ev", 0)]
                + [("kpT_ev", g) for g in range(4)] + [("vT_ev", g) for g in range(4)]
                + [x for g in range(4) for x in (("num_ev", g), ("den_ev", g))])
        A_NAC = len(A_AC)
        A_IAC = {k: i for i, k in enumerate(A_AC)}

        def acA(h, key):
            return h * A_NAC + A_IAC[key] + 1

        esa = es.enter_context(contextlib.ExitStack())
        sba = lambda name, shape, dt: esa.enter_context(nc.sbuf_tensor(name, shape, dt))
        wq_bf = [sba(f"wq_bf{i}", [128, 2048], BF16) for i in range(2)]
        _wk1 = sba("wk_bf0", [128, 2048], BF16)
        wk_bf = [_wk1, _wk1]
        _wv1 = sba("wv_bf0", [128, 2048], BF16)
        wv_bf = [_wv1, _wv1]
        kp_hd = sba("kp_hd", [128, 2048], BF16)
        v_hd = sba("v_hd", [128, 2048], BF16)
        kp_tok = sba("kp_tok", [128, 2048], BF16)
        vc_aug = sba("vc_aug", [128, NCH * 129], BF16)
        s_prev = sba("s_prev", [128, NCH * 129], BF16)
        acc = sba("acc", [128, 129], F32)
        scoresT = sba("scoresT", [128, 2048], BF16)
        num_bf = [sba(f"num_bf{i}", [128, 2048], BF16) for i in range(2)]
        den_sb = [sba(f"den_sb{i}", [1, 2048], BF16) for i in range(2)]
        tmpb = [sba(f"tmpb{i}", [128, 512], BF16) for i in range(2)]
        tmpc = [sba(f"tmpc{i}", [128, 512], BF16) for i in range(2)]

        psA = [esa.enter_context(nc.psum_tensor(f"psA{i}", [128, 512], F32)) for i in range(2)]
        psT = esa.enter_context(nc.psum_tensor("psT", [128, 512], F32))
        psKV = [esa.enter_context(nc.psum_tensor(f"psKV{i}", [128, 129], F32)) for i in range(2)]
        psS = esa.enter_context(nc.psum_tensor("psS", [128, 512], F32))
        psN = esa.enter_context(nc.psum_tensor("psN", [128, 512], F32))
        psD = esa.enter_context(nc.psum_tensor("psD", [1, 512], F32))

        @blk.gpsimd
        def _(gpsimd):
            for q in range(4):
                gpsimd.dma_start(
                    out=x_bf[:, q * 4 * TPC:(q + 1) * 4 * TPC].rearrange(
                        "p (n t) -> p n t", t=TPC),
                    in_=x_h[q * 512:(q + 1) * 512, :].rearrange(
                        "(n p) t -> p n t", p=128)).then_inc(s_xq[q].h, 16)
            va3 = vc_aug[:, :].rearrange("p (n c) -> p n c", c=129)
            gpsimd.memset(va3[:, :, 128:129], 1.0).then_inc(s_pool.h, 1)
            for h in range(H):
                if h == min((H - H2S) + 1, H - 1):
                    gpsimd.wait_ge(s_bld.h, 32)
                    gpsimd.collective_compute(
                        "AllGather", ALU.bypass,
                        replica_groups=[list(range(NCORES))],
                        ins=[cc_in_a.ap()[:, :].opt()],
                        outs=[cc_out_a.ap()[:, :].opt()],
                    ).then_inc(s_cc.h, 1)
                if h > 1:
                    gpsimd.wait_ge(s_pe.h, peA(h - 2, ("qsum", 0)))
                gpsimd.dma_start(out=wq_bf[h % 2][:, :],
                                 in_=wq_h[h, :, :]).then_inc(s_wq[h % 2].h, 16)
                if h > 0:
                    gpsimd.wait_ge(s_pe.h, peA(h - 1, ("k", NS - 1)))
                gpsimd.dma_start(out=wk_bf[h % 2][:, :], in_=wk_h[h, :, :]).then_inc(s_wk.h, 16)
                if h > 0:
                    gpsimd.wait_ge(s_pe.h, peA(h - 1, ("v", NS - 1)))
                gpsimd.dma_start(out=wv_bf[h % 2][:, :], in_=wv_h[h, :, :]).then_inc(s_wv.h, 16)
                if h > 0:
                    gpsimd.wait_ge(s_act.h, acA(h - 1, ("num_ev", 3)))
                    gpsimd.dma_start(out=sp_num[h - 1, :, :],
                                     in_=num_bf[(h - 1) % 2][:, :]).then_inc(s_spn[(h - 1) % 2].h, 16)
                    gpsimd.wait_ge(s_act.h, acA(h - 1, ("den_ev", 3)))
                    gpsimd.dma_start(out=sp_den[h - 1, :, :],
                                     in_=den_sb[(h - 1) % 2][:, :]).then_inc(s_spd[(h - 1) % 2].h, 16)
            gpsimd.wait_ge(s_act.h, acA(H - 1, ("num_ev", 3)))
            gpsimd.dma_start(out=sp_num[H - 1, :, :],
                             in_=num_bf[(H - 1) % 2][:, :]).then_inc(s_spn[(H - 1) % 2].h, 16)
            gpsimd.wait_ge(s_act.h, acA(H - 1, ("den_ev", 3)))
            gpsimd.dma_start(out=sp_den[H - 1, :, :],
                             in_=den_sb[(H - 1) % 2][:, :]).then_inc(s_spd[(H - 1) % 2].h, 16)
        for q in range(4):
            s_xq[q].n = 16
        s_pool.n = 1

        @blk.vector
        def _(vector):
            for dc in range(DC):
                if dc % 4 == 0:
                    vector.wait_ge(s_xq[dc // 4].h, 16)
                s_dve.inc(vector.reduce_sum(out=xsum[:, dc:dc + 1],
                                            in_=x_bf[:, dc * TPC:(dc + 1) * TPC], axis=AX.X))
            vector.wait_ge(s_dve.h, 16)
            s_dve.inc(vector.tensor_copy(out=xsum_bf[:, :], in_=xsum[:, :]))

        @blk.tensor
        def _(tensor):
            tensor.wait_ge(s_cld.h, CLD)
            for h in range(H):
                tensor.wait_ge(s_wq[h % 2].h, (h // 2 + 1) * 16)
                for s in range(NS):
                    if s >= 2:
                        tensor.wait_ge(s_dve.h, dvA(h, ("qstt", s - 2)))
                    elif h > 0:
                        tensor.wait_ge(s_act.h, acA(h - 1, ("vcp", s + 2)))
                    for dc in range(DC):
                        if h == 0 and s == 0 and dc % 4 == 0:
                            tensor.wait_ge(s_xq[dc // 4].h, 16)
                        mm = tensor.matmul(psA[s % 2][:, :],
                                           wq_bf[h % 2][:, dc * 128:(dc + 1) * 128],
                                           x_bf[:, dc * TPC + s * SW: dc * TPC + (s + 1) * SW],
                                           start=(dc == 0), stop=(dc == DC - 1))
                    s_pe.inc(mm)
                if h > 0:
                    tensor.wait_ge(s_act.h, acA(h - 1, ("vT_ev", 3)))
                tensor.wait_ge(s_dve.h, DVE0)
                for dc in range(DC):
                    mm = tensor.matmul(psT[:, 0:1], wq_bf[h % 2][:, dc * 128:(dc + 1) * 128],
                                       xsum_bf[:, dc:dc + 1],
                                       start=(dc == 0), stop=(dc == DC - 1))
                s_pe.inc(mm)
                tensor.wait_ge(s_wk.h, (h + 1) * 16)
                for s in range(NS):
                    if s < 2:
                        tensor.wait_ge(s_dve.h, dvA(h, ("qstt", s + 2)))
                    else:
                        tensor.wait_ge(s_dve.h, dvA(h, ("kstt", s - 2)))
                    for dc in range(DC):
                        mm = tensor.matmul(psA[s % 2][:, :],
                                           wk_bf[h % 2][:, dc * 128:(dc + 1) * 128],
                                           x_bf[:, dc * TPC + s * SW: dc * TPC + (s + 1) * SW],
                                           start=(dc == 0), stop=(dc == DC - 1))
                    s_pe.inc(mm)
                tensor.wait_ge(s_wv.h, (h + 1) * 16)
                for s in range(NS):
                    if s < 2:
                        tensor.wait_ge(s_dve.h, dvA(h, ("kstt", s + 2)))
                    else:
                        tensor.wait_ge(s_act.h, acA(h, ("vcp", s - 2)))
                    for dc in range(DC):
                        mm = tensor.matmul(psA[s % 2][:, :],
                                           wv_bf[h % 2][:, dc * 128:(dc + 1) * 128],
                                           x_bf[:, dc * TPC + s * SW: dc * TPC + (s + 1) * SW],
                                           start=(dc == 0), stop=(dc == DC - 1))
                    s_pe.inc(mm)
                tensor.wait_ge(s_act.h, acA(h, ("qsum_ev", 0)))
                tensor.wait_ge(s_dve.h, dvA(h, ("kstt", 3)))
                tbank = [psT, psS]
                for g in range(4):
                    if g >= 2:
                        tensor.wait_ge(s_act.h, acA(h, ("kpT_ev", g - 2)))
                    for n4 in range(4):
                        n = 4 * g + n4
                        mm = tensor.matmul(tbank[g % 2][:, n4 * 128:(n4 + 1) * 128],
                                           kp_hd[:, n * 128:(n + 1) * 128],
                                           ident[:, :], start=True, stop=True)
                    s_pe.inc(mm)
                tensor.wait_ge(s_act.h, acA(h, ("vcp", 3)))
                for g in range(4):
                    if g < 2:
                        tensor.wait_ge(s_act.h, acA(h, ("kpT_ev", g + 2)))
                    else:
                        tensor.wait_ge(s_act.h, acA(h, ("vT_ev", g - 2)))
                    for n4 in range(4):
                        n = 4 * g + n4
                        mm = tensor.matmul(tbank[g % 2][:, n4 * 128:(n4 + 1) * 128],
                                           v_hd[:, n * 128:(n + 1) * 128],
                                           ident[:, :], start=True, stop=True)
                    s_pe.inc(mm)
                if h == 0:
                    tensor.wait_ge(s_pool.h, 1)

                def emit_kv_group(g):
                    tensor.wait_ge(s_act.h, acA(h, ("kpT_ev", g)))
                    tensor.wait_ge(s_act.h, acA(h, ("vT_ev", g)))
                    for n4 in range(4):
                        n = 4 * g + n4
                        if n >= 2:
                            tensor.wait_ge(s_dve.h, dvA(h, ("kvadd", n - 2)))
                        elif h > 0:
                            tensor.wait_ge(s_dve.h, dvA(h - 1, ("kvadd", n + 14)))
                        mm = tensor.matmul(psKV[n % 2][:, :],
                                           kp_tok[:, n * 128:(n + 1) * 128],
                                           vc_aug[:, n * 129:(n + 1) * 129],
                                           start=True, stop=True)
                        s_pe.inc(mm)

                def emit_scT(g):
                    if g > 0:
                        tensor.wait_ge(s_dve.h, dvA(h, ("scm", g - 1)))
                    elif h > 0:
                        tensor.wait_ge(s_dve.h, dvA(h - 1, ("scm", 3)))
                    tensor.wait_ge(s_act.h, acA(h, ("vT_ev", 3)))
                    tensor.wait_ge(s_dve.h, dvA(h, ("qstt", g)))
                    for n4 in range(4):
                        n = 4 * g + n4
                        mm = tensor.matmul(psS[:, n4 * 128:(n4 + 1) * 128],
                                           kp_hd[:, n * 128:(n + 1) * 128],
                                           qp_all[:, h * TPC + n * 128: h * TPC + (n + 1) * 128],
                                           start=True, stop=True)
                    s_pe.inc(mm)

                def emit_num(g):
                    tensor.wait_ge(s_dve.h, dvA(h, ("kvcp", 4 * g + 3)))
                    tensor.wait_ge(s_dve.h, dvA(h, ("scm", g)))
                    if g > 0:
                        tensor.wait_ge(s_act.h, acA(h, ("num_ev", g - 1)))
                    elif h > 0:
                        tensor.wait_ge(s_act.h, acA(h - 1, ("num_ev", 3)))
                    for n4 in range(4):
                        n = 4 * g + n4
                        tensor.matmul(psN[:, n4 * 128:(n4 + 1) * 128],
                                      s_prev[:, n * 129:n * 129 + 128],
                                      qp_all[:, h * TPC + n * 128: h * TPC + (n + 1) * 128],
                                      start=True, stop=False)
                        mm = tensor.matmul(psN[:, n4 * 128:(n4 + 1) * 128],
                                           vc_aug[:, n * 129:n * 129 + 128],
                                           scoresT[:, n * 128:(n + 1) * 128],
                                           start=False, stop=True)
                    s_pe.inc(mm)

                def emit_den(g):
                    if g > 0:
                        tensor.wait_ge(s_act.h, acA(h, ("den_ev", g - 1)))
                    elif h > 0:
                        tensor.wait_ge(s_act.h, acA(h - 1, ("den_ev", 3)))
                    for n4 in range(4):
                        n = 4 * g + n4
                        tensor.matmul(psD[:, n4 * 128:(n4 + 1) * 128],
                                      s_prev[:, n * 129 + 128:(n + 1) * 129],
                                      qp_all[:, h * TPC + n * 128: h * TPC + (n + 1) * 128],
                                      start=True, stop=False)
                        mm = tensor.matmul(psD[:, n4 * 128:(n4 + 1) * 128],
                                           onesc[:, :],
                                           scoresT[:, n * 128:(n + 1) * 128],
                                           start=False, stop=True)
                    s_pe.inc(mm)

                for g in range(4):
                    emit_kv_group(g)
                    emit_scT(g)
                    if g >= 1:
                        emit_num(g - 1)
                        emit_den(g - 1)
                emit_num(3)
                emit_den(3)

        @blk.vector
        def _(vector):
            for h in range(H):
                if h > 0:
                    vector.wait_ge(s_dve.h, dvA(h - 1, ("kvtot", 0)))
                s_dve.inc(vector.memset(acc[:, :], 0.0))
                for s in range(NS):
                    vector.wait_ge(s_act.h, acA(h, ("qexp", s)))
                    if s >= 2:
                        vector.wait_ge(s_dve.h, dvA(h, ("qstt", s - 2)))
                    elif h > 0:
                        vector.wait_ge(s_dve.h, dvA(h - 1, ("kstt", s + 2)))
                    v = vector.tensor_scalar_min(out=tmpc[s % 2][:, :], in0=tmpb[s % 2][:, :],
                                                 scalar1=1.0)
                    s_dve.inc(v)
                    vector.wait_ge(s_dve.h, dvA(h, ("qmin", s)))
                    v = vector.scalar_tensor_tensor(
                        out=qp_all[:, h * TPC + s * SW: h * TPC + (s + 1) * SW],
                        in0=psA[s % 2][:, :], scalar=0.0, in1=tmpc[s % 2][:, :],
                        op0=ALU.max, op1=ALU.add)
                    s_dve.inc(v)
                for s in range(NS):
                    vector.wait_ge(s_act.h, acA(h, ("kexp", s)))
                    if s >= 2:
                        vector.wait_ge(s_dve.h, dvA(h, ("kstt", s - 2)))
                    else:
                        vector.wait_ge(s_dve.h, dvA(h, ("qstt", s + 2)))
                    v = vector.tensor_scalar_min(out=tmpc[s % 2][:, :], in0=tmpb[s % 2][:, :],
                                                 scalar1=1.0)
                    s_dve.inc(v)
                    vector.wait_ge(s_dve.h, dvA(h, ("kmin", s)))
                    v = vector.scalar_tensor_tensor(
                        out=kp_hd[:, s * SW:(s + 1) * SW],
                        in0=psA[s % 2][:, :], scalar=0.0, in1=tmpc[s % 2][:, :],
                        op0=ALU.max, op1=ALU.add)
                    s_dve.inc(v)
                if h == 0:
                    vector.wait_ge(s_cld.h, CLD)
                for g in range(4):
                    for n4 in range(4):
                        n = 4 * g + n4
                        vector.wait_ge(s_pe.h, peA(h, ("kv", n)))
                        if n > 0:
                            vector.wait_ge(s_dve.h, dvA(h, ("kvadd", n - 1)))
                        else:
                            vector.wait_ge(s_dve.h, dvA(h, ("ms", 0)))
                        v = vector.tensor_copy(out=s_prev[:, n * 129:(n + 1) * 129],
                                               in_=acc[:, :])
                        s_dve.inc(v)
                        vector.wait_ge(s_dve.h, dvA(h, ("kvcp", n)))
                        v = vector.tensor_tensor(out=acc[:, :], in0=psKV[n % 2][:, :],
                                                 in1=acc[:, :], op=ALU.add)
                        s_dve.inc(v)
                    vector.wait_ge(s_pe.h, peA(h, ("scT", g)))
                    if h > 0:
                        vector.wait_ge(s_pe.h, peA(h - 1, ("den", g)))
                    v = vector.tensor_tensor(out=scoresT[:, g * 512:(g + 1) * 512],
                                             in0=psS[:, :], in1=mask4[:, :], op=ALU.mult)
                    s_dve.inc(v)
                vector.wait_ge(s_dve.h, dvA(h, ("kvadd", 15)))
                v = vector.tensor_copy(out=kvtot[:, h * 129:(h + 1) * 129], in_=acc[:, :])
                s_dve.inc(v)

        @blk.scalar
        def _(scalar):
            for h in range(H):
                for s in range(NS):
                    scalar.wait_ge(s_pe.h, peA(h, ("q", s)))
                    if s >= 2:
                        scalar.wait_ge(s_dve.h, dvA(h, ("qmin", s - 2)))
                    elif h > 0:
                        scalar.wait_ge(s_dve.h, dvA(h - 1, ("kmin", s + 2)))
                    a = scalar.activation(out=tmpb[s % 2][:, :], in_=psA[s % 2][:, :],
                                          func=AF.Exp)
                    s_act.inc(a)
                for s in range(NS):
                    scalar.wait_ge(s_pe.h, peA(h, ("k", s)))
                    scalar.wait_ge(s_dve.h, dvA(h, ("qmin", s + 2) if s < 2 else ("kmin", s - 2)))
                    a = scalar.activation(out=tmpb[s % 2][:, :], in_=psA[s % 2][:, :],
                                          func=AF.Exp)
                    s_act.inc(a)
                for s in range(NS):
                    scalar.wait_ge(s_pe.h, peA(h, ("v", s)))
                    a = scalar.copy(out=v_hd[:, s * SW:(s + 1) * SW], in_=psA[s % 2][:, :])
                    s_act.inc(a)
                scalar.wait_ge(s_pe.h, peA(h, ("qsum", 0)))
                a = scalar.copy(out=qsum_all[:, h:h + 1], in_=psT[:, 0:1])
                s_act.inc(a)
                for g in range(4):
                    scalar.wait_ge(s_pe.h, peA(h, ("kpT", g)))
                    a = scalar.copy(out=kp_tok[:, g * 512:(g + 1) * 512],
                                    in_=(psT if g % 2 == 0 else psS)[:, :])
                    s_act.inc(a)
                va3 = vc_aug[:, :].rearrange("p (n c) -> p n c", c=129)
                for g in range(4):
                    scalar.wait_ge(s_pe.h, peA(h, ("vT", g)))
                    src3 = (psT if g % 2 == 0 else psS)[:, :].rearrange(
                        "p (n c) -> p n c", c=128)
                    a = scalar.copy(out=va3[:, 4 * g:4 * g + 4, 0:128], in_=src3)
                    s_act.inc(a)
                for g in range(4):
                    scalar.wait_ge(s_pe.h, peA(h, ("num", g)))
                    if h >= 2:
                        scalar.wait_ge(s_spn[h % 2].h, 16 * (h // 2))
                    a = scalar.copy(out=num_bf[h % 2][:, g * 512:(g + 1) * 512], in_=psN[:, :])
                    s_act.inc(a)
                    scalar.wait_ge(s_pe.h, peA(h, ("den", g)))
                    if h >= 2:
                        scalar.wait_ge(s_spd[h % 2].h, 16 * (h // 2))
                    a = scalar.copy(out=den_sb[h % 2][:, g * 512:(g + 1) * 512], in_=psD[:, :])
                    s_act.inc(a)

        if dbg:
            @blk.sync
            def _(sync):
                sync.wait_ge(s_dve.h, dvA(H - 1, ("kvtot", 0)))
                sync.wait_ge(s_act.h, acA(H - 1, ("den_ev", 3)))
                for h in range(H):
                    sync.dma_start(out=dbg_qp[h, :, :],
                                   in_=qp_all[:, h * TPC:(h + 1) * TPC]).then_inc(s_out.h, 16)
                sync.dma_start(out=dbg_kvtot[:, :], in_=kvtot[:, :]).then_inc(s_out.h, 16)
                sync.dma_start(out=dbg_qsum[:, :], in_=qsum_all[:, :]).then_inc(s_out.h, 16)
            s_out.n += (H + 2) * 16

        if want <= stages.index("A"):
            @blk.sync
            def _(sync):
                sync.wait_ge(s_spn[0].h, 16 * (H - H // 2))
                sync.wait_ge(s_spn[1].h, 16 * (H // 2))
                sync.wait_ge(s_spd[0].h, 16 * (H - H // 2))
                sync.wait_ge(s_spd[1].h, 16 * (H // 2))
                if dbg:
                    sync.wait_ge(s_out.h, s_out.n)
            return nc

        esa.close()
        esAx.close()

        PE_A_END = H * A_NPE
        DV_A_END = DVE0 + H * A_NDV
        AC_A_END = H * A_NAC

        B0_DV = [("ms_soff", 0), ("ms_qsg", 0), ("ms_cb", 0)]
        for i in range(2 * NCORES):
            B0_DV += [("soff_r", i), ("qsg_r", i)]
        B0_DV += [("soff_bf", 0), ("ms_zblk", 0)]
        B0_DV += [("zcp", h) for h in range(H)]
        B0_DV += [("qsg_bf", 0), ("sig_add1", 0), ("sig_recip", 0), ("omg_row", 0)]
        B0_DV += [("nmev", s) for s in range(NS)]
        B0_DV += [("beta_recip", 0)]
        B0_DV += [("dcomb", s) for s in range(NS)]
        B0_DV += [("dmax", 0), ("drecip", 0), ("domg", 0), ("cb_bf", 0)]
        B0_DV += [("tot_red", 0), ("empty", 0), ("asum", 0), ("allm", 0),
                  ("em64", 0), ("em64m", 0), ("relmask", 0), ("e30", 0), ("relm", 0),
                  ("mx_red", 0), ("relsub", 0),
                  ("ssum", 0), ("srecip", 0), ("w64", 0), ("almf", 0), ("wfin", 0),
                  ("wg_row", 0)]
        B0_IDV = {k: i for i, k in enumerate(B0_DV)}

        def dvB0(key):
            return DV_A_END + B0_IDV[key] + 1

        DV_B0_END = DV_A_END + len(B0_DV)

        B0_AC = [("eg_exp", 0), ("tot_ev", 0), ("rel_ev", 0), ("omg_ev", 0),
                 ("esoft", 0), ("wgbc_ev", 0)]
        B0_IAC = {k: i for i, k in enumerate(B0_AC)}

        def acB0(key):
            return AC_A_END + B0_IAC[key] + 1

        AC_B0_END = AC_A_END + len(B0_AC)

        B0_PE = ([("nm", s) for s in range(NS)] + [("do", s) for s in range(NS)]
                 + [("tot", 0), ("rel", 0), ("omgT", 0), ("wgbc", 0)])
        B0_IPE = {k: i for i, k in enumerate(B0_PE)}

        def peB0(key):
            return PE_A_END + B0_IPE[key] + 1

        PE_B0_END = PE_A_END + len(B0_PE)

        B_PE = []
        for s in range(NS):
            B_PE += [("io", s), ("bc", s, 0), ("bc", s, 1), ("bc", s, 2),
                     ("bc", s, 3), ("ac", s, 0), ("bcr", s), ("ac", s, 1),
                     ("ac", s, 2), ("ac", s, 3)]
        B_NPE = len(B_PE)
        B_IPE = {k: i for i, k in enumerate(B_PE)}

        def peB(h, key):
            return PE_B0_END + h * B_NPE + B_IPE[key] + 1

        B_DV = []
        for s in range(NS):
            B_DV += [("qpm", s, 0), ("qpm", s, 1), ("qpm", s, 2), ("qpm", s, 3),
                     ("nf", s), ("loc", s), ("out", s)]
        B_NDV = len(B_DV)
        B_IDV = {k: i for i, k in enumerate(B_DV)}

        def dvB(h, key):
            return DV_B0_END + h * B_NDV + B_IDV[key] + 1

        GLD_B = 5 * 16

        esb = es.enter_context(contextlib.ExitStack())
        sbb = lambda name, shape, dt: esb.enter_context(nc.sbuf_tensor(name, shape, dt))
        soff = sbb("soff", [128, H * 129], F32)
        qsg = sbb("qsg", [128, H], F32)
        recv = [sbb(f"recv{i}", [128, (H - H2S) * 130], F32) for i in range(2)]
        soff_bf = sbb("soff_bf", [128, H * 129], BF16)
        zblk = sbb("zblk", [128, H * 16], BF16)
        qsg_bf = sbb("qsg_bf", [128, H], BF16)
        mnorm_bf = sbb("mnorm_bf", [128, H * M], BF16)
        mem_bfsb = sbb("mem_bfsb", [128, H * M * 128], BF16)
        mem_gsb = mem_bfsb
        mnormblk_bf = sbb("mnormblk_bf", [128, H * 64], BF16)
        sel80 = sbb("sel80", [80, 80 * 128], BF16)
        cb = sbb("cb", [80, TPC], F32)
        cb_bf = sbb("cb_bf", [80, TPC], BF16)
        den_bfs = sbb("den_bfs", [H, TPC], BF16)
        dfull = sbb("dfull", [H, TPC], F32)
        eg = sbb("eg", [1, H], F32)
        sig = sbb("sig", [1, H], F32)
        omg_row = sbb("omg_row", [1, H], F32)
        omg_col = sbb("omg_col", [H, 1], F32)
        tot_row = sbb("tot_row", [1, H * M], F32)
        tot4 = sbb("tot4", [1, M], F32)
        empt = sbb("empt", [1, M], F32)
        asum = sbb("asum", [1, 1], F32)
        allm = sbb("allm", [1, 1], F32)
        almf = sbb("almf", [1, 1], F32)
        em64 = sbb("em64", [1, H * M], F32)
        em64m = sbb("em64m", [1, H * M], F32)
        e30 = sbb("e30", [1, H * M], F32)
        rel_row = sbb("rel_row", [1, H * M], F32)
        relmask = sbb("relmask", [1, H * M], F32)
        relm = sbb("relm", [1, H * M], F32)
        relsub = sbb("relsub", [1, H * M], F32)
        e64 = sbb("e64", [1, H * M], F32)
        mx16 = sbb("mx16", [1, H], F32)
        s16 = sbb("s16", [1, H], F32)
        sr16 = sbb("sr16", [1, H], F32)
        w64 = sbb("w64", [1, H * M], F32)
        wfin = sbb("wfin", [1, H * M], F32)
        wg_row = sbb("wg_row", [1, H * M], F32)
        wgbc_sb = sbb("wgbc_sb", [128, H * M], F32)
        num_in = [sbb(f"num_in{i}", [128, TPC], BF16) for i in range(2)]
        qpm_bf = [sbb(f"qpm_bf{i}", [128, SW], BF16) for i in range(2)]
        tmpn = [sbb(f"tmpn{i}", [128, SW], F32) for i in range(2)]
        tmpl = [sbb(f"tmpl{i}", [128, SW], F32) for i in range(2)]
        attn_t = [sbb(f"attn_t{i}", [128, TPC], BF16) for i in range(2)]

        s_rcv = [mksem("s_rcv0"), mksem("s_rcv1")]
        s_nin = [mksem("s_nin0"), mksem("s_nin1")]
        s_sana = [mksem("s_sana0"), mksem("s_sana1")]
        s_sanb = [mksem("s_sanb0"), mksem("s_sanb1")]
        HA8 = min(8, H)
        SANA = [sum(1 for j in range(HA8) if j % 2 == p) * 16 for p in (0, 1)]
        SANB = [sum(1 for j in range(8, H) if j % 2 == p) * 16 for p in (0, 1)]

        esb_pro = es.enter_context(contextlib.ExitStack())
        psNm = esb_pro.enter_context(nc.psum_tensor("psNm", [64, 512], F32))
        psDo = esb_pro.enter_context(nc.psum_tensor("psDo", [16, 512], F32))
        psR = esb_pro.enter_context(nc.psum_tensor("psR", [1, H * M], F32))
        psW = esb_pro.enter_context(nc.psum_tensor("psW", [128, H * M], F32))
        psOm = esb_pro.enter_context(nc.psum_tensor("psOm", [H, 1], F32))
        psTot = esb_pro.enter_context(nc.psum_tensor("psTot", [1, H * M], F32))

        HA = H - H2
        @blk.sync
        def _(sync):
            sync.wait_ge(s_dve.h, dvA(HA - 1, ("kvtot", 0)))
            sync.wait_ge(s_act.h, acA(HA - 1, ("qsum_ev", 0)))
            sync.dma_start(out=cc_in_a.ap()[:, 0:HA * 129],
                           in_=kvtot[:, 0:HA * 129]).then_inc(s_bld.h, 16)
            sync.dma_start(out=cc_in_a.ap()[:, HA * 129:HA * 129 + HA],
                           in_=qsum_all[:, 0:HA],
                           allow_slow_non_contiguous=True).then_inc(s_bld.h, 16)
            sync.wait_ge(s_dve.h, dvA(H - 1, ("kvtot", 0)))
            sync.wait_ge(s_act.h, acA(H - 1, ("qsum_ev", 0)))
            sync.dma_start(out=cc_in_b.ap()[:, 0:H2 * 129],
                           in_=kvtot[:, HA * 129:H * 129]).then_inc(s_bld.h, 16)
            sync.dma_start(out=cc_in_b.ap()[:, H2 * 129:H2 * 129 + H2],
                           in_=qsum_all[:, HA:H],
                           allow_slow_non_contiguous=True).then_inc(s_bld.h, 16)
        s_bld.n = 64

        @blk.gpsimd
        def _(gpsimd):
            gpsimd.wait_ge(s_bld.h, 64)
            gpsimd.collective_compute(
                "AllGather", ALU.bypass,
                replica_groups=[list(range(NCORES))],
                ins=[cc_in_b.ap()[:, :].opt()],
                outs=[cc_out_b.ap()[:, :].opt()],
            ).then_inc(s_ccb.h, 1)
            s_ccb.n = 1
            s_cc.n = 1
            gpsimd.wait_ge(s_pe.h, PE_A_END)
            gpsimd.wait_ge(s_dve.h, DV_A_END)
            gpsimd.wait_ge(s_act.h, AC_A_END)
            gpsimd.wait_ge(s_spn[0].h, 16 * (H - H // 2))
            gpsimd.wait_ge(s_spn[1].h, 16 * (H // 2))
            gpsimd.wait_ge(s_spd[0].h, 16 * (H - H // 2))
            gpsimd.wait_ge(s_spd[1].h, 16 * (H // 2))
            gpsimd.dma_start(out=mem_bfsb[:, :], in_=mem_h[:, :]).then_inc(s_gld.h, 16)
            gpsimd.dma_start(
                out=mnormblk_bf[:, :].rearrange("p (h c) -> p h c", c=64),
                in_=mnormblk_h[:, :, :].rearrange("h p c -> p h c")
            ).then_inc(s_gld.h, 16)
            gpsimd.dma_start(
                out=mnorm_bf[:, :].rearrange("p (h m) -> p h m", m=M),
                in_=mnormfm_h[:, :, :].rearrange("h p m -> p h m")
            ).then_inc(s_gld.h, 16)
            gpsimd.dma_start(out=sel80[:, :], in_=sel80_h[:, :]).then_inc(s_gld.h, 16)
            gpsimd.dma_start(
                out=den_bfs[:, :], in_=sp_den[:, 0, :]).then_inc(s_gld.h, 16)
        s_gld.n = GLD_B

        @blk.sync
        def _(sync):
            sync.wait_ge(s_cc.h, 1)
            sync.wait_ge(s_pe.h, PE_A_END)
            sync.wait_ge(s_act.h, AC_A_END)
            sync.wait_ge(s_spn[0].h, 16 * (H - H // 2))
            sync.wait_ge(s_spn[1].h, 16 * (H // 2))
            sync.wait_ge(s_spd[0].h, 16 * (H - H // 2))
            sync.wait_ge(s_spd[1].h, 16 * (H // 2))
            for hf in range(2):
                sync.wait_ge((s_cc if hf == 0 else s_ccb).h, 1)
                cco = cc_out_a if hf == 0 else cc_out_b
                w = (H - H2) * 130 if hf == 0 else H2 * 130
                for r in range(NCORES):
                    i = hf * NCORES + r
                    if i >= 2:
                        sync.wait_ge(s_dve.h, dvB0(("qsg_r", i - 2)))
                    sync.dma_start(out=recv[i % 2][:, 0:w],
                                   in_=cco.ap()[r * 128:(r + 1) * 128, :]).then_inc(
                        s_rcv[i % 2].h, 16)

        @blk.vector
        def _(vector):
            vector.wait_ge(s_pe.h, PE_A_END)
            vector.wait_ge(s_dve.h, DV_A_END)
            vector.wait_ge(s_act.h, AC_A_END)
            vector.wait_ge(s_spn[0].h, 16 * (H - H // 2))
            vector.wait_ge(s_spn[1].h, 16 * (H // 2))
            vector.wait_ge(s_spd[0].h, 16 * (H - H // 2))
            vector.wait_ge(s_spd[1].h, 16 * (H // 2))
            s_dve.inc(vector.memset(soff[:, :], 0.0))
            s_dve.inc(vector.memset(qsg[:, :], 0.0))
            s_dve.inc(vector.memset(cb[:, :], 0.0))
            HA2 = H - H2
            for hf in range(2):
                nh = HA2 if hf == 0 else H2
                scol = 0 if hf == 0 else HA2 * 129
                qcol = 0 if hf == 0 else HA2
                for r in range(NCORES):
                    i = hf * NCORES + r
                    vector.wait_ge(s_rcv[i % 2].h, 16 * (i // 2 + 1))
                    if i > 0:
                        vector.wait_ge(s_dve.h, dvB0(("qsg_r", i - 1)))
                    else:
                        vector.wait_ge(s_dve.h, dvB0(("ms_qsg", 0)))
                    v = vector.scalar_tensor_tensor(
                        out=soff[:, scol:scol + nh * 129],
                        in0=recv[i % 2][:, 0:nh * 129],
                        scalar=maskcc[:, r:r + 1], in1=soff[:, scol:scol + nh * 129],
                        op0=ALU.mult, op1=ALU.add)
                    s_dve.inc(v)
                    v = vector.scalar_tensor_tensor(
                        out=qsg[:, qcol:qcol + nh],
                        in0=recv[i % 2][:, nh * 129:nh * 129 + nh],
                        scalar=maskcc[:, 8 + r:9 + r], in1=qsg[:, qcol:qcol + nh],
                        op0=ALU.mult, op1=ALU.add)
                    s_dve.inc(v)
            vector.wait_ge(s_dve.h, dvB0(("soff_r", 2 * NCORES - 1)))
            s_dve.inc(vector.tensor_copy(out=soff_bf[:, :], in_=soff[:, :]))
            s_dve.inc(vector.memset(zblk[:, :], 0.0))
            for h in range(H):
                vector.wait_ge(s_dve.h, dvB0(("ms_zblk", 0)))
                v = vector.tensor_copy(out=zblk[:, h * 16 + h:h * 16 + h + 1],
                                       in_=soff[:, h * 129 + 128:(h + 1) * 129])
                s_dve.inc(v)
            vector.wait_ge(s_dve.h, dvB0(("qsg_r", NCORES - 1)))
            s_dve.inc(vector.tensor_copy(out=qsg_bf[:, :], in_=qsg[:, :]))
            vector.wait_ge(s_act.h, acB0(("eg_exp", 0)))
            s_dve.inc(vector.tensor_scalar_add(out=sig[:, :], in0=eg[:, :], scalar1=1.0))
            vector.wait_ge(s_dve.h, dvB0(("sig_add1", 0)))
            s_dve.inc(vector.reciprocal(out=sig[:, :], in_=sig[:, :]))
            vector.wait_ge(s_dve.h, dvB0(("sig_recip", 0)))
            s_dve.inc(vector.tensor_scalar(out=omg_row[:, :], in0=sig[:, :],
                                           scalar1=-1.0, scalar2=1.0,
                                           op0=ALU.mult, op1=ALU.add))
            for s in range(NS):
                vector.wait_ge(s_pe.h, peB0(("nm", s)))
                vector.wait_ge(s_dve.h, dvB0(("ms_cb", 0)))
                v = vector.tensor_scalar_max(out=cb[0:64, s * SW:(s + 1) * SW],
                                             in0=psNm[:, :], scalar1=EPS)
                s_dve.inc(v)
            vector.wait_ge(s_dve.h, dvB0(("nmev", NS - 1)))
            s_dve.inc(vector.reciprocal(out=cb[0:64, :], in_=cb[0:64, :]))
            for s in range(NS):
                vector.wait_ge(s_pe.h, peB0(("do", s)))
                vector.wait_ge(s_gld.h, GLD_B)
                v = vector.scalar_tensor_tensor(out=dfull[:, s * SW:(s + 1) * SW],
                                                in0=psDo[0:H, :], scalar=0.0,
                                                in1=den_bfs[:, s * SW:(s + 1) * SW],
                                                op0=ALU.bypass, op1=ALU.add)
                s_dve.inc(v)
            vector.wait_ge(s_dve.h, dvB0(("dcomb", NS - 1)))
            s_dve.inc(vector.tensor_scalar_max(out=dfull[:, :], in0=dfull[:, :], scalar1=EPS))
            vector.wait_ge(s_dve.h, dvB0(("dmax", 0)))
            s_dve.inc(vector.reciprocal(out=dfull[:, :], in_=dfull[:, :]))
            vector.wait_ge(s_dve.h, dvB0(("drecip", 0)))
            vector.wait_ge(s_act.h, acB0(("omg_ev", 0)))
            s_dve.inc(vector.tensor_scalar(out=cb[64:64 + H, :], in0=dfull[:, :],
                                           scalar1=omg_col[:, :], scalar2=None,
                                           op0=ALU.mult))
            vector.wait_ge(s_dve.h, dvB0(("domg", 0)))
            vector.wait_ge(s_dve.h, dvB0(("beta_recip", 0)))
            s_dve.inc(vector.tensor_copy(out=cb_bf[:, :], in_=cb[:, :]))
            vector.wait_ge(s_act.h, acB0(("tot_ev", 0)))
            t3 = tot_row[:, :].rearrange("p (h m) -> p m h", m=M)
            s_dve.inc(vector.reduce_sum(out=tot4[:, :].rearrange("p (m o) -> p m o", o=1),
                                        in_=t3, axis=AX.X))
            vector.wait_ge(s_dve.h, dvB0(("tot_red", 0)))
            s_dve.inc(vector.tensor_single_scalar(out=empt[:, :], in_=tot4[:, :],
                                                  scalar=1e-6, op=ALU.is_lt))
            vector.wait_ge(s_dve.h, dvB0(("empty", 0)))
            s_dve.inc(vector.reduce_sum(out=asum[:, :], in_=empt[:, :], axis=AX.X))
            vector.wait_ge(s_dve.h, dvB0(("asum", 0)))
            s_dve.inc(vector.tensor_single_scalar(out=allm[:, :], in_=asum[:, :],
                                                  scalar=3.5, op=ALU.is_gt))
            emb, _ = bass.broadcast_tensor_aps(
                empt[:, :].rearrange("p (o m) -> p o m", o=1),
                em64[:, :].rearrange("p (h m) -> p h m", m=M))
            s_dve.inc(vector.tensor_copy(out=em64[:, :].rearrange("p (h m) -> p h m", m=M),
                                         in_=emb))
            vector.wait_ge(s_dve.h, dvB0(("em64", 0)))
            s_dve.inc(vector.tensor_scalar(out=em64m[:, :], in0=em64[:, :],
                                           scalar1=-1.0, scalar2=1.0,
                                           op0=ALU.mult, op1=ALU.add))
            vector.wait_ge(s_act.h, acB0(("rel_ev", 0)))
            vector.wait_ge(s_dve.h, dvB0(("em64m", 0)))
            s_dve.inc(vector.tensor_mul(out=relmask[:, :], in0=rel_row[:, :],
                                        in1=em64m[:, :]))
            s_dve.inc(vector.tensor_scalar_mul(out=e30[:, :], in0=em64[:, :],
                                               scalar1=-1.0e30))
            vector.wait_ge(s_dve.h, dvB0(("e30", 0)))
            s_dve.inc(vector.tensor_add(out=relm[:, :], in0=relmask[:, :],
                                        in1=e30[:, :]))
            vector.wait_ge(s_dve.h, dvB0(("relm", 0)))
            r3 = relm[:, :].rearrange("p (h m) -> p h m", m=M)
            s_dve.inc(vector.reduce_max(out=mx16[:, :].rearrange("p (h o) -> p h o", o=1),
                                        in_=r3, axis=AX.X))
            vector.wait_ge(s_dve.h, dvB0(("mx_red", 0)))
            r3b, mxb = bass.broadcast_tensor_aps(
                relm[:, :].rearrange("p (h m) -> p h m", m=M),
                mx16[:, :].rearrange("p (h o) -> p h o", o=1))
            s_dve.inc(vector.tensor_tensor(
                out=relsub[:, :].rearrange("p (h m) -> p h m", m=M),
                in0=r3b, in1=mxb, op=ALU.subtract))
            vector.wait_ge(s_act.h, acB0(("esoft", 0)))
            e3 = e64[:, :].rearrange("p (h m) -> p h m", m=M)
            s_dve.inc(vector.reduce_sum(out=s16[:, :].rearrange("p (h o) -> p h o", o=1),
                                        in_=e3, axis=AX.X))
            vector.wait_ge(s_dve.h, dvB0(("ssum", 0)))
            s_dve.inc(vector.reciprocal(out=sr16[:, :], in_=s16[:, :]))
            vector.wait_ge(s_dve.h, dvB0(("srecip", 0)))
            e3b, srb = bass.broadcast_tensor_aps(
                e64[:, :].rearrange("p (h m) -> p h m", m=M),
                sr16[:, :].rearrange("p (h o) -> p h o", o=1))
            s_dve.inc(vector.tensor_tensor(
                out=w64[:, :].rearrange("p (h m) -> p h m", m=M),
                in0=e3b, in1=srb, op=ALU.mult))
            vector.wait_ge(s_dve.h, dvB0(("allm", 0)))
            s_dve.inc(vector.tensor_scalar(out=almf[:, :], in0=allm[:, :],
                                           scalar1=-1.0, scalar2=1.0,
                                           op0=ALU.mult, op1=ALU.add))
            vector.wait_ge(s_dve.h, dvB0(("w64", 0)))
            vector.wait_ge(s_dve.h, dvB0(("almf", 0)))
            s_dve.inc(vector.tensor_scalar(out=wfin[:, :], in0=w64[:, :],
                                           scalar1=almf[:, :], scalar2=None,
                                           op0=ALU.mult))
            vector.wait_ge(s_dve.h, dvB0(("wfin", 0)))
            w3, sgb = bass.broadcast_tensor_aps(
                wfin[:, :].rearrange("p (h m) -> p h m", m=M),
                sig[:, :].rearrange("p (h o) -> p h o", o=1))
            s_dve.inc(vector.tensor_tensor(
                out=wg_row[:, :].rearrange("p (h m) -> p h m", m=M),
                in0=w3, in1=sgb, op=ALU.mult))

        @blk.scalar
        def _(scalar):
            scalar.wait_ge(s_pe.h, PE_A_END)
            scalar.wait_ge(s_act.h, AC_A_END)
            scalar.wait_ge(s_dve.h, DV_A_END)
            scalar.wait_ge(s_spn[0].h, 16 * (H - H // 2))
            scalar.wait_ge(s_spn[1].h, 16 * (H // 2))
            scalar.wait_ge(s_spd[0].h, 16 * (H - H // 2))
            scalar.wait_ge(s_spd[1].h, 16 * (H // 2))
            a = scalar.activation(out=eg[:, :], in_=gate_sb[:, :], func=AF.Exp, scale=-1.0)
            s_act.inc(a)
            scalar.wait_ge(s_pe.h, peB0(("tot", 0)))
            s_act.inc(scalar.copy(out=tot_row[:, :], in_=psTot[:, :]))
            scalar.wait_ge(s_pe.h, peB0(("rel", 0)))
            a = scalar.activation(out=rel_row[:, :], in_=psR[:, :], func=AF.Copy,
                                  scale=float(SCALE / S))
            s_act.inc(a)
            scalar.wait_ge(s_pe.h, peB0(("omgT", 0)))
            s_act.inc(scalar.copy(out=omg_col[:, :], in_=psOm[:, :]))
            scalar.wait_ge(s_dve.h, dvB0(("relsub", 0)))
            s_act.inc(scalar.activation(out=e64[:, :], in_=relsub[:, :], func=AF.Exp))
            scalar.wait_ge(s_pe.h, peB0(("wgbc", 0)))
            s_act.inc(scalar.copy(out=wgbc_sb[:, :], in_=psW[:, :]))

        @blk.vector
        def _(vector):
            vector.wait_ge(s_gld.h, GLD_B)
            vector.wait_ge(s_act.h, acB0(("wgbc_ev", 0)))
            for h in range(H):
                for m in range(M):
                    i = h * M + m
                    p = vector.tensor_scalar(
                        out=mem_bfsb[:, i * 128:(i + 1) * 128],
                        in0=mem_bfsb[:, i * 128:(i + 1) * 128],
                        scalar1=wgbc_sb[:, i:i + 1], scalar2=None, op0=ALU.mult)
                p.then_inc(s_pool.h, 1)
        MEMG_BASE = s_pool.n
        POOL_MEMG = MEMG_BASE + H
        s_pool.n = POOL_MEMG

        @blk.tensor
        def _(tensor):
            tensor.wait_ge(s_dve.h, DV_A_END)
            tensor.wait_ge(s_act.h, AC_A_END)
            tensor.wait_ge(s_gld.h, GLD_B)
            for s in range(NS):
                for h in range(H):
                    mm = tensor.matmul(psNm[:, :],
                                       mnormblk_bf[:, h * 64:(h + 1) * 64],
                                       qp_all[:, h * TPC + s * SW: h * TPC + (s + 1) * SW],
                                       start=(h == 0), stop=(h == H - 1))
                if s > 0:
                    pass
                s_pe.inc(mm)
                if s < NS - 1:
                    tensor.wait_ge(s_dve.h, dvB0(("nmev", s)))
            tensor.wait_ge(s_dve.h, dvB0(("zcp", H - 1)))
            for s in range(NS):
                for h in range(H):
                    mm = tensor.matmul(psDo[:, :],
                                       zblk[:, h * 16:(h + 1) * 16],
                                       qp_all[:, h * TPC + s * SW: h * TPC + (s + 1) * SW],
                                       start=(h == 0), stop=(h == H - 1))
                s_pe.inc(mm)
                if s < NS - 1:
                    tensor.wait_ge(s_dve.h, dvB0(("dcomb", s)))
            mm = tensor.matmul(psTot[:, :], onesc[:, :], mnorm_bf[:, :],
                               start=True, stop=True)
            s_pe.inc(mm)
            tensor.wait_ge(s_dve.h, dvB0(("qsg_bf", 0)))
            for h in range(H):
                mm = tensor.matmul(psR[:, h * 4:(h + 1) * 4],
                                   qsg_bf[:, h:h + 1],
                                   mnorm_bf[:, h * 4:(h + 1) * 4],
                                   start=True, stop=True)
            s_pe.inc(mm)
            tensor.wait_ge(s_dve.h, dvB0(("omg_row", 0)))
            mm = tensor.matmul(psOm[:, :], omg_row[:, :], ones128[0:1, 0:1],
                               start=True, stop=True)
            s_pe.inc(mm)
            tensor.wait_ge(s_dve.h, dvB0(("wg_row", 0)))
            mm = tensor.matmul(psW[:, :], ones128[0:1, :], wg_row[:, :],
                               start=True, stop=True)
            s_pe.inc(mm)

        esb_pro.close()
        psIO = [esb.enter_context(nc.psum_tensor(f"psIO{i}", [128, 512], F32)) for i in range(2)]
        psBC = [esb.enter_context(nc.psum_tensor(f"psBC{i}", [128, 512], F32)) for i in range(4)]
        psAC = [esb.enter_context(nc.psum_tensor(f"psAC{i}", [128, 512], F32)) for i in range(2)]

        @blk.sync
        def _(sync):
            for h in range(H):
                if h >= 2:
                    sync.wait_ge(s_dve.h, dvB(h - 2, ("nf", NS - 1)))
                sync.dma_start(out=num_in[h % 2][:, :],
                               in_=sp_num[h, :, :]).then_inc(s_nin[h % 2].h, 16)

        @blk.tensor
        def _(tensor):
            tensor.wait_ge(s_act.h, acB0(("wgbc_ev", 0)))
            tensor.wait_ge(s_dve.h, dvB0(("dcomb", NS - 1)))
            for h in range(H):
                tensor.wait_ge(s_pool.h, MEMG_BASE + h + 1)
                for s in range(NS):
                    k = h * NS + s
                    qps = qp_all[:, h * TPC + s * SW: h * TPC + (s + 1) * SW]
                    if k >= 2:
                        tensor.wait_ge(s_dve.h, dvB((k - 2) // NS, ("nf", (k - 2) % NS)))
                    mm = tensor.matmul(psIO[k % 2][:, :],
                                       soff_bf[:, h * 129:h * 129 + 128],
                                       qps, start=True, stop=True)
                    s_pe.inc(mm)
                    if k >= 1:
                        tensor.wait_ge(s_dve.h, dvB((k - 1) // NS, ("loc", (k - 1) % NS)))
                    mm = tensor.matmul(psBC[0][:, :],
                                       sel80[:, (4 * h + 0) * 128:(4 * h + 1) * 128],
                                       cb_bf[:, s * SW:(s + 1) * SW], start=True, stop=True)
                    s_pe.inc(mm)
                    for m in range(1, 4):
                        if k >= 1:
                            tensor.wait_ge(s_dve.h, dvB((k - 1) // NS,
                                                        ("qpm", (k - 1) % NS, m)))
                        mm = tensor.matmul(psBC[m][:, :],
                                           sel80[:, (4 * h + m) * 128:(4 * h + m + 1) * 128],
                                           cb_bf[:, s * SW:(s + 1) * SW],
                                           start=True, stop=True)
                        s_pe.inc(mm)
                    if k >= 2:
                        tensor.wait_ge(s_dve.h, dvB((k - 2) // NS, ("out", (k - 2) % NS)))
                    tensor.wait_ge(s_dve.h, dvB(h, ("qpm", s, 0)))
                    mm = tensor.matmul(psAC[k % 2][:, :],
                                       mem_gsb[:, (h * M + 0) * 128:(h * M + 1) * 128],
                                       qpm_bf[0][:, :], start=True, stop=False)
                    s_pe.inc(mm)
                    mm = tensor.matmul(psBC[0][:, :],
                                       sel80[:, (64 + h) * 128:(64 + h + 1) * 128],
                                       cb_bf[:, s * SW:(s + 1) * SW], start=True, stop=True)
                    s_pe.inc(mm)
                    for m in range(1, 4):
                        tensor.wait_ge(s_dve.h, dvB(h, ("qpm", s, m)))
                        mm = tensor.matmul(psAC[k % 2][:, :],
                                           mem_gsb[:, (h * M + m) * 128:(h * M + m + 1) * 128],
                                           qpm_bf[m % 2][:, :], start=False,
                                           stop=(m == 3))
                        s_pe.inc(mm)

        @blk.vector
        def _(vector):
            for h in range(H):
                for s in range(NS):
                    k = h * NS + s
                    qps = qp_all[:, h * TPC + s * SW: h * TPC + (s + 1) * SW]
                    for m in range(4):
                        vector.wait_ge(s_pe.h, peB(h, ("bc", s, m)))
                        if m >= 2:
                            vector.wait_ge(s_pe.h, peB(h, ("ac", s, m - 2)))
                        elif k >= 1:
                            vector.wait_ge(s_pe.h, peB((k - 1) // NS,
                                                       ("ac", (k - 1) % NS, m + 2)))
                        v = vector.scalar_tensor_tensor(
                            out=qpm_bf[m % 2][:, :], in0=psBC[m][:, :], scalar=0.0,
                            in1=qps, op0=ALU.bypass, op1=ALU.mult)
                        s_dve.inc(v)
                    vector.wait_ge(s_pe.h, peB(h, ("io", s)))
                    vector.wait_ge(s_nin[h % 2].h, 16 * (h // 2 + 1))
                    if k >= 2:
                        vector.wait_ge(s_dve.h, dvB((k - 2) // NS, ("loc", (k - 2) % NS)))
                    v = vector.scalar_tensor_tensor(
                        out=tmpn[k % 2][:, :], in0=psIO[k % 2][:, :], scalar=0.0,
                        in1=num_in[h % 2][:, s * SW:(s + 1) * SW],
                        op0=ALU.bypass, op1=ALU.add)
                    s_dve.inc(v)
                    vector.wait_ge(s_pe.h, peB(h, ("bcr", s)))
                    vector.wait_ge(s_dve.h, dvB(h, ("nf", s)))
                    if k >= 2:
                        vector.wait_ge(s_dve.h, dvB((k - 2) // NS, ("out", (k - 2) % NS)))
                    v = vector.tensor_tensor(out=tmpl[k % 2][:, :], in0=psBC[0][:, :],
                                             in1=tmpn[k % 2][:, :], op=ALU.mult)
                    s_dve.inc(v)
                    vector.wait_ge(s_pe.h, peB(h, ("ac", s, 3)))
                    vector.wait_ge(s_dve.h, dvB(h, ("loc", s)))
                    if h >= 2 and s == 0:
                        j = h - 2
                        if j < 8:
                            vector.wait_ge(s_sana[j % 2].h, 16 * (j // 2 + 1))
                        else:
                            vector.wait_ge(s_sanb[j % 2].h, 16 * ((j - 8) // 2 + 1))
                    v = vector.scalar_tensor_tensor(
                        out=attn_t[h % 2][:, s * SW:(s + 1) * SW],
                        in0=psAC[k % 2][:, :], scalar=0.0, in1=tmpl[k % 2][:, :],
                        op0=ALU.bypass, op1=ALU.add)
                    s_dve.inc(v)

        @blk.gpsimd
        def _(gpsimd):
            for h in range(H):
                gpsimd.wait_ge(s_dve.h, dvB(h, ("out", NS - 1)))
                gpsimd.dma_start(out=sp_attn[h, :, :],
                                 in_=attn_t[h % 2][:, :]).then_inc(
                    (s_sana if h < 8 else s_sanb)[h % 2].h, 16)

        if dbg:
            @blk.sync
            def _(sync):
                sync.wait_ge(s_dve.h, DV_B0_END)
                sync.dma_start(out=dbg_soff[:, :], in_=soff[:, :]).then_inc(s_out.h, 16)
                sync.dma_start(out=dbg_cb[:, :], in_=cb[:, :]).then_inc(s_out.h, 16)
                sync.dma_start(out=dbg_wg[:, :], in_=wg_row[:, :]).then_inc(s_out.h, 16)
            s_out.n += 3 * 16

        if want <= stages.index("B"):
            @blk.sync
            def _(sync):
                for p in (0, 1):
                    sync.wait_ge(s_sana[p].h, SANA[p])
                    sync.wait_ge(s_sanb[p].h, SANB[p])
                if dbg:
                    sync.wait_ge(s_out.h, s_out.n)
            return nc

        esb.close()
        esAB.close()

        PE_B_END = PE_B0_END + H * B_NPE
        DV_B_END = DV_B0_END + H * B_NDV
        AC_B_END = AC_B0_END


        L_PE = []
        for fc in range(16):
            L_PE += [("ymm", fc, 0), ("ymm", fc, 1)]
            if fc >= 1:
                L_PE += [("sts", fc - 1, 0), ("stq", fc - 1, 0),
                         ("sts", fc - 1, 1), ("stq", fc - 1, 1)]
        L_PE += [("sts", 15, 0), ("stq", 15, 0), ("sts", 15, 1), ("stq", 15, 1)]
        L_PE += [("bcr", 0), ("bcr", 1), ("bcn", 0), ("bcn", 1)]
        L_NPE = len(L_PE)
        L_IPE = {k: i for i, k in enumerate(L_PE)}

        def peL(th, key):
            return PE_B_END + th * L_NPE + L_IPE[key] + 1

        L_DV = []
        for fc in range(16):
            L_DV += [("ye", fc, 0), ("yc", fc, 0), ("ye", fc, 1), ("yc", fc, 1)]
        for sl in range(2):
            L_DV += [("nm", sl), ("msq", sl), ("mm2", sl), ("var", sl), ("nmr", sl)]
        for fc in range(16):
            L_DV += [("xnm", fc), ("xna", fc)]
            if fc % 2 == 0:
                L_DV += [("xc", fc)]
        L_NDV = len(L_DV)
        L_IDV = {k: i for i, k in enumerate(L_DV)}

        def dvL(th, key):
            return DV_B_END + th * L_NDV + L_IDV[key] + 1

        L_AC = []
        for fc in range(16):
            L_AC += [("sq", fc, 0), ("sq", fc, 1)]
        for sl in range(2):
            L_AC += [("lv", sl), ("rstd", sl)]
        L_AC += [("rbc_ev", 0), ("nbc_ev", 0)]
        L_NAC = len(L_AC)
        L_IAC = {k: i for i, k in enumerate(L_AC)}

        def acL(th, key):
            return AC_B_END + th * L_NAC + L_IAC[key] + 1

        esf1 = es.enter_context(contextlib.ExitStack())
        x1bf = esf1.enter_context(nc.sbuf_tensor("x1bf", [128, 16 * TPC], BF16))
        esl = es.enter_context(contextlib.ExitStack())
        sbl = lambda name, shape, dt: esl.enter_context(nc.sbuf_tensor(name, shape, dt))
        attn_hf = esl.enter_context(
            nc.sbuf_tensor("attn_hf", [128, H * 1024], BF16, side="right"))
        wo_bf = [sbl(f"wo_bf{i}", [128, 2048], BF16) for i in range(2)]
        x0_sb = [sbl(f"x0_sb{i}", [128, 1024], F32) for i in range(2)]
        y1h = [sbl(f"y1h{i}", [128, 1024], F32) for i in range(16)]
        ybf_t = [sbl(f"ybf_t{i}", [128, 512], BF16) for i in range(2)]
        sqbf_t = [sbl(f"sqbf_t{i}", [128, 512], BF16) for i in range(2)]
        _nm2r = sbl("nm_row", [33, 512], F32)
        _msq2r = sbl("msq_row", [33, 512], F32)
        _var2r = sbl("var_row", [33, 512], F32)
        _rstd2r = sbl("rstd_row", [33, 512], F32)
        _nmr2r = sbl("nmr_row", [33, 512], F32)
        nm_row = [_nm2r[0:1, :], _nm2r[32:33, :]]
        msq_row = [_msq2r[0:1, :], _msq2r[32:33, :]]
        var_row = [_var2r[0:1, :], _var2r[32:33, :]]
        rstd_row = [_rstd2r[0:1, :], _rstd2r[32:33, :]]
        nmr_row = [_nmr2r[0:1, :], _nmr2r[32:33, :]]
        rstd_bc = sbl("rstd_bc", [128, 1024], F32)
        nmr_bc = sbl("nmr_bc", [128, 1024], F32)
        psY = [esl.enter_context(nc.psum_tensor(f"psY{i}", [128, 512], F32)) for i in range(2)]
        psSum = [esl.enter_context(nc.psum_tensor(f"psSum{i}", [1, 512], F32)) for i in range(2)]
        psSq = [esl.enter_context(nc.psum_tensor(f"psSq{i}", [1, 512], F32)) for i in range(2)]
        psBcL = esl.enter_context(nc.psum_tensor("psBcL", [128, 1024], F32))

        s_lat = mksem("s_lat")
        s_lwo = [mksem("s_lwo0"), mksem("s_lwo1")]
        s_lx0 = [mksem("s_lx00"), mksem("s_lx01")]
        s_sx1f = mksem("s_sx1f")
        s_sp8 = [mksem(f"s_sp8{k}") for k in range(8)]

        def lnbarrier(eng):
            eng.wait_ge(s_pe.h, PE_B_END)
            eng.wait_ge(s_dve.h, DV_B_END)
            eng.wait_ge(s_act.h, AC_B_END)
            eng.wait_ge(s_pool.h, POOL_MEMG)
            for p in (0, 1):
                eng.wait_ge(s_sana[p].h, SANA[p])
                eng.wait_ge(s_sanb[p].h, SANB[p])

        @blk.sync
        def _(sync):
            lnbarrier(sync)
            for p in (0, 1):
                sync.wait_ge(s_sana[p].h, SANA[p])
            sync.dma_start(
                out=attn_hf[:, 0:HA8 * 1024].rearrange("p (h t) -> p h t", t=1024),
                in_=sp_attn[0:HA8, :, 0:1024].rearrange(
                    "h p t -> p h t")).then_inc(s_lat.h, 16)
            def x0_load(th, fc):
                if th * 16 + fc >= 2:
                    pfc, pth = (fc - 2, th) if fc >= 2 else (14 + fc, th - 1)
                    sync.wait_ge(s_dve.h, dvL(pth, ("ye", pfc, 1)))
                sync.dma_start(out=x0_sb[fc % 2][:, :],
                               in_=x_h[fc * 128:(fc + 1) * 128,
                                       th * 1024:(th + 1) * 1024]).then_inc(
                    s_lx0[fc % 2].h, 16)

            if H > 8:
                for p in (0, 1):
                    sync.wait_ge(s_sanb[p].h, SANB[p])
                sync.dma_start(
                    out=attn_hf[:, 8 * 1024:H * 1024].rearrange(
                        "p (h t) -> p h t", t=1024),
                    in_=sp_attn[8:H, :, 0:1024].rearrange(
                        "h p t -> p h t")).then_inc(s_lat.h, 16)
            NLAT0 = 2 if H > 8 else 1
            for fc in range(16):
                x0_load(0, fc)
            sync.wait_ge(s_pe.h, peL(0, ("ymm", 15, 1)))
            sync.dma_start(
                out=attn_hf[:, :].rearrange("p (h t) -> p h t", t=1024),
                in_=sp_attn[:, :, 1024:2048].rearrange(
                    "h p t -> p h t")).then_inc(s_lat.h, 16)
            NLAT1 = NLAT0 + 1
            x0_load(1, 0)
            x0_load(1, 1)
            for fc in range(16):
                sync.wait_ge(s_dve.h, dvL(0, ("xna", fc)))
                if fc >= 1:
                    sync.wait_ge(s_pe.h, peL(1, ("ymm", min(fc - 1, 13), 1)))
                sync.dma_start(out=sp_x1[fc, :, 0:1024],
                               in_=y1h[fc][:, :]).then_inc(s_sx1f.h, 16)
                if fc <= 13:
                    x0_load(1, fc + 2)
            for fc in range(16):
                sync.wait_ge(s_dve.h, dvL(1, ("xna", fc)))
                sync.dma_start(out=sp_x1[fc, :, 1024:2048],
                               in_=y1h[fc][:, :]).then_inc(s_sp8[fc // 2].h, 16)

        @blk.gpsimd
        def _(gpsimd):
            lnbarrier(gpsimd)
            def wo_load(th, fc):
                if th * 16 + fc >= 2:
                    pfc, pth = (fc - 2, th) if fc >= 2 else (14 + fc, th - 1)
                    gpsimd.wait_ge(s_pe.h, peL(pth, ("ymm", pfc, 1)))
                gpsimd.dma_start(out=wo_bf[fc % 2][:, :],
                                 in_=wo_h[fc, :, :]).then_inc(s_lwo[fc % 2].h, 16)

            for fc in range(16):
                wo_load(0, fc)
            wo_load(1, 0)
            wo_load(1, 1)
            for fc in range(16):
                if fc % 2 == 1:
                    gpsimd.wait_ge(s_dve.h, dvL(0, ("xna", fc)))
                    p = gpsimd.tensor_copy(
                        out=x1bf[:, fc * TPC: fc * TPC + 1024],
                        in_=y1h[fc][:, :])
                    p.then_inc(s_pool.h, 1)
                if fc <= 13:
                    wo_load(1, fc + 2)


        @blk.tensor
        def _(tensor):
            tensor.wait_ge(s_dve.h, DV_B_END)
            tensor.wait_ge(s_act.h, AC_B_END)
            for th in range(2):
                tensor.wait_ge(s_lat.h, 16 * ((2 if H > 8 else 1) + th))
                def emit_stats(fc):
                    for sl in range(2):
                        z = fc * 2 + sl
                        tensor.wait_ge(s_dve.h, dvL(th, ("yc", fc, sl)))
                        if fc == 0 and th > 0:
                            tensor.wait_ge(s_dve.h, dvL(th - 1, ("msq", sl)))
                        mm = tensor.matmul(psSum[sl][:, :], onesc[:, :],
                                           ybf_t[z % 2][:, :],
                                           start=(fc == 0), stop=(fc == 15))
                        s_pe.inc(mm)
                        tensor.wait_ge(s_act.h, acL(th, ("sq", fc, sl)))
                        mm = tensor.matmul(psSq[sl][:, :], onesc[:, :],
                                           sqbf_t[z % 2][:, :],
                                           start=(fc == 0), stop=(fc == 15))
                        s_pe.inc(mm)

                for fc in range(16):
                    tensor.wait_ge(s_lwo[fc % 2].h, 16 * (th * 8 + fc // 2 + 1))
                    for sl in range(2):
                        kk = th * 16 + fc
                        if kk >= 1:
                            pfc, pth = (fc - 1, th) if fc >= 1 else (15, th - 1)
                            tensor.wait_ge(s_dve.h, dvL(pth, ("ye", pfc, sl)))
                        for hc in range(H):
                            mm = tensor.matmul(
                                psY[sl][:, :],
                                wo_bf[fc % 2][:, hc * 128:(hc + 1) * 128],
                                attn_hf[:, hc * 1024 + sl * 512: hc * 1024 + (sl + 1) * 512],
                                start=(hc == 0), stop=(hc == H - 1))
                        s_pe.inc(mm)
                    if fc >= 1:
                        emit_stats(fc - 1)
                emit_stats(15)
                for sl in range(2):
                    tensor.wait_ge(s_dve.h, dvL(th, ("nmr", sl)))
                    if th > 0:
                        tensor.wait_ge(s_act.h, acL(th - 1, ("nbc_ev", 0)))
                    mm = tensor.matmul(psBcL[:, sl * 512:(sl + 1) * 512],
                                       ones128[32 * sl:32 * sl + 1, :], rstd_row[sl][:, :],
                                       start=True, stop=True)
                    s_pe.inc(mm)
                tensor.wait_ge(s_act.h, acL(th, ("rbc_ev", 0)))
                for sl in range(2):
                    mm = tensor.matmul(psBcL[:, sl * 512:(sl + 1) * 512],
                                       ones128[32 * sl:32 * sl + 1, :], nmr_row[sl][:, :],
                                       start=True, stop=True)
                    s_pe.inc(mm)

        @blk.vector
        def _(vector):
            lnbarrier(vector)
            for th in range(2):
                for fc in range(16):
                    if th == 1:
                        vector.wait_ge(s_sx1f.h, 16 * (fc + 1 if fc <= 13 else 16))
                        if fc % 2 == 1:
                            vector.wait_ge(s_pool.h, POOL_MEMG + fc // 2 + 1)
                    vector.wait_ge(s_lx0[fc % 2].h, 16 * (th * 8 + fc // 2 + 1))
                    for sl in range(2):
                        vector.wait_ge(s_pe.h, peL(th, ("ymm", fc, sl)))
                        v = vector.scalar_tensor_tensor(
                            out=y1h[fc][:, sl * 512:(sl + 1) * 512],
                            in0=psY[sl][:, :], scalar=0.0,
                            in1=x0_sb[fc % 2][:, sl * 512:(sl + 1) * 512],
                            op0=ALU.bypass, op1=ALU.add)
                        s_dve.inc(v)
                        z = th * 32 + fc * 2 + sl
                        if z >= 2:
                            pz = z - 2
                            pth, prem = divmod(pz, 32)
                            pfc, psl = divmod(prem, 2)
                            vector.wait_ge(s_pe.h, peL(pth, ("sts", pfc, psl)))
                        vector.wait_ge(s_dve.h, dvL(th, ("ye", fc, sl)))
                        v = vector.tensor_copy(out=ybf_t[(fc * 2 + sl) % 2][:, :],
                                               in_=y1h[fc][:, sl * 512:(sl + 1) * 512])
                        s_dve.inc(v)
                for sl in range(2):
                    vector.wait_ge(s_pe.h, peL(th, ("sts", 15, sl)))
                    v = vector.tensor_scalar_mul(out=nm_row[sl][:, :], in0=psSum[sl][:, :],
                                                 scalar1=-1.0 / D)
                    s_dve.inc(v)
                    vector.wait_ge(s_pe.h, peL(th, ("stq", 15, sl)))
                    v = vector.tensor_scalar_mul(out=msq_row[sl][:, :], in0=psSq[sl][:, :],
                                                 scalar1=1.0 / D)
                    s_dve.inc(v)
                    vector.wait_ge(s_dve.h, dvL(th, ("nm", sl)))
                    v = vector.tensor_mul(out=var_row[sl][:, :], in0=nm_row[sl][:, :],
                                          in1=nm_row[sl][:, :])
                    s_dve.inc(v)
                    vector.wait_ge(s_dve.h, dvL(th, ("mm2", sl)))
                    vector.wait_ge(s_dve.h, dvL(th, ("msq", sl)))
                    v = vector.tensor_sub(out=var_row[sl][:, :], in0=msq_row[sl][:, :],
                                          in1=var_row[sl][:, :])
                    s_dve.inc(v)
                    vector.wait_ge(s_act.h, acL(th, ("rstd", sl)))
                    v = vector.tensor_mul(out=nmr_row[sl][:, :], in0=nm_row[sl][:, :],
                                          in1=rstd_row[sl][:, :])
                    s_dve.inc(v)
                for fc in range(16):
                    vector.wait_ge(s_act.h, acL(th, ("nbc_ev", 0)))
                    vector.wait_ge(s_dve.h, dvL(th, ("yc", fc, 1)))
                    v = vector.tensor_mul(out=y1h[fc][:, :], in0=y1h[fc][:, :],
                                          in1=rstd_bc[:, :])
                    s_dve.inc(v)
                    vector.wait_ge(s_dve.h, dvL(th, ("xnm", fc)))
                    v = vector.tensor_add(out=y1h[fc][:, :], in0=y1h[fc][:, :],
                                          in1=nmr_bc[:, :])
                    s_dve.inc(v)
                    if fc % 2 == 0:
                        vector.wait_ge(s_dve.h, dvL(th, ("xna", fc)))
                        v = vector.tensor_copy(
                            out=x1bf[:, fc * TPC + th * 1024: fc * TPC + (th + 1) * 1024],
                            in_=y1h[fc][:, :])
                        s_dve.inc(v)

        @blk.scalar
        def _(scalar):
            lnbarrier(scalar)
            for th in range(2):
                for fc in range(16):
                    for sl in range(2):
                        scalar.wait_ge(s_dve.h, dvL(th, ("ye", fc, sl)))
                        z = th * 32 + fc * 2 + sl
                        if z >= 2:
                            pth, prem = divmod(z - 2, 32)
                            pfc, psl = divmod(prem, 2)
                            scalar.wait_ge(s_pe.h, peL(pth, ("stq", pfc, psl)))
                        a = scalar.activation(out=sqbf_t[(fc * 2 + sl) % 2][:, :],
                                              in_=y1h[fc][:, sl * 512:(sl + 1) * 512],
                                              func=AF.Square)
                        s_act.inc(a)
                for sl in range(2):
                    scalar.wait_ge(s_dve.h, dvL(th, ("var", sl)))
                    a = scalar.activation(out=var_row[sl][:, :], in_=var_row[sl][:, :],
                                          func=AF.Ln, bias=eps_sb[0:1, 0:1])
                    s_act.inc(a)
                    scalar.wait_ge(s_act.h, acL(th, ("lv", sl)))
                    a = scalar.activation(out=rstd_row[sl][:, :], in_=var_row[sl][:, :],
                                          func=AF.Exp, scale=-0.5)
                    s_act.inc(a)
                scalar.wait_ge(s_pe.h, peL(th, ("bcr", 1)))
                s_act.inc(scalar.copy(out=rstd_bc[:, :], in_=psBcL[:, :]))
                scalar.wait_ge(s_pe.h, peL(th, ("bcn", 1)))
                s_act.inc(scalar.copy(out=nmr_bc[:, :], in_=psBcL[:, :]))

        POOL_L_END = POOL_MEMG + 16
        s_pool.n = POOL_L_END

        PE_L_END = PE_B_END + 2 * L_NPE
        DV_L_END = DV_B_END + 2 * L_NDV
        AC_L_END = AC_B_END + 2 * L_NAC
        SX1 = 16 * 16

        if want <= stages.index("ln1"):
            @blk.sync
            def _(sync):
                sync.wait_ge(s_sx1f.h, SX1)
                for k in range(8):
                    sync.wait_ge(s_sp8[k].h, 32)
                if dbg:
                    sync.wait_ge(s_out.h, s_out.n)
            return nc

        esl.close()

        F_SEQ = [(0, ("vec", ac)) for ac in range(8)]
        for s in range(NS):
            F_SEQ += [(s, ("arr", g4)) for g4 in range(4)]
            F_SEQ += [(s, ("rt", g4)) for g4 in range(4)]
            if s + 1 < NS:
                F_SEQ += [(s + 1, ("vec", 0)), (s + 1, ("vec", 1))]
            F_SEQ += [x for ac in range(8)
                      for x in ((s, ("rbc", ac)), (s, ("vsum", ac)))]
            F_SEQ += [(s, ("vbc", 0))]
            if s + 1 < NS:
                F_SEQ += [(s + 1, ("vec", ac)) for ac in range(2, 8)]
        F_ISEQ = {k: i for i, k in enumerate(F_SEQ)}

        def peF(s, key):
            return PE_L_END + F_ISEQ[(s, key)] + 1

        F_DV = []
        for ac in range(8):
            F_DV += [("vce", ac)]
        for step in range(11):
            for g4 in range(4):
                F_DV += [(f"t4_{step}", g4) if step < 10 else ("rts", g4)]
        F_DV += [("rtev", 0)]
        for ac in range(8):
            F_DV += [("u1", ac)]
        for ac in range(8):
            F_DV += [("u2", ac)]
        F_NDV = len(F_DV)
        F_IDV = {k: i for i, k in enumerate(F_DV)}

        def dvF(s, key):
            return DV_L_END + s * F_NDV + F_IDV[key] + 1

        F_AC = []
        for ac in range(8):
            F_AC += [("sqv", ac)]
        F_AC += [("arr_ev", 0)]
        for ac in range(8):
            F_AC += [("rbce", ac)]
        F_AC += [("v16ev", 0)]
        F_NAC = len(F_AC)
        F_IAC = {k: i for i, k in enumerate(F_AC)}

        def acF(s, key):
            return AC_L_END + s * F_NAC + F_IAC[key] + 1

        PE_F_END = PE_L_END + len(F_SEQ)
        DV_F_END = DV_L_END + NS * F_NDV
        AC_F_END = AC_L_END + NS * F_NAC

        E_PE_ORD = []
        for jci in range(8):
            E_PE_ORD.append(("emm", jci))
            if jci >= 1:
                E_PE_ORD += [("es", jci - 1), ("eq", jci - 1)]
        E_PE_ORD += [("es", 7), ("eq", 7)]
        E_IPE = {k: i for i, k in enumerate(E_PE_ORD)}

        def peE(g, s, jci, kind):
            base = PE_F_END + (g * NS + s) * 24
            return base + E_IPE[(kind, jci)] + 1

        def dvE(g, s, item):
            base = DV_F_END + (g * NS + s) * 18
            if item[0] == "ye2":
                return base + item[1] * 2 + 1
            if item[0] == "yc2":
                return base + item[1] * 2 + 2
            if item[0] == "sadd":
                return base + 17
            return base + 18

        def acE(g, s, jci):
            return AC_F_END + (g * NS + s) * 8 + jci + 1

        PE_E_END = PE_F_END + 2 * NS * 24
        DV_E_END = DV_F_END + 2 * NS * 18
        AC_E_END = AC_F_END + 2 * NS * 8

        def peN(i):
            return PE_E_END + i + 1

        LN2_DV = ["nm", "msq", "mm2", "var", "nmr"] + \
                 [x for jc in range(0, 16, 2) for x in (f"xm{jc}", f"xa{jc}")]
        LN2_IDV = {k: i for i, k in enumerate(LN2_DV)}

        def dvN(key):
            return DV_E_END + LN2_IDV[key] + 1

        LN2_PL = [x for jc in range(1, 16, 2) for x in (f"xm{jc}", f"xa{jc}")]
        LN2_IPL = {k: i for i, k in enumerate(LN2_PL)}

        def plN(key):
            return POOL_L_END + LN2_IPL[key] + 1

        LN2_AC = ["lv", "rstd", "rbc_ev", "nbc_ev"]
        LN2_IAC = {k: i for i, k in enumerate(LN2_AC)}

        def acN(key):
            return AC_E_END + LN2_IAC[key] + 1

        OUT_BASE = s_out.n
        sp_y2 = ap("sp_y2", [16, 128, TPC], F32, "Internal" if not dbg else "ExternalOutput")

        esf = es.enter_context(contextlib.ExitStack())
        sbf = lambda name, shape, dt: esf.enter_context(
            nc.sbuf_tensor(name, shape, dt, side="right"))
        u2f = sbf("u2f", [128, 8 * TPC], BF16)
        y2l = [sbf(f"y2l{i}", [128, TPC], F32) for i in range(2)]

        s_fxb = mksem("s_fxb")
        s_fcl = mksem("s_fcl")
        s_fg = mksem("s_fg")
        s_fa = mksem("s_fa")
        s_wfg = [mksem("s_wfga"), mksem("s_wfgb")]
        s_b2g = mksem("s_b2g")
        s_x1g = [mksem("s_x1g0"), mksem("s_x1g1")]
        s_y2s = [[mksem(f"s_y2s{g}{i}") for i in range(2)] for g in range(2)]
        s_y2l = [mksem("s_y2l0"), mksem("s_y2l1")]
        s_o2 = [mksem("s_o20"), mksem("s_o21")]

        def ffbarrier(eng):
            eng.wait_ge(s_pe.h, PE_L_END)
            eng.wait_ge(s_dve.h, DV_L_END)
            eng.wait_ge(s_act.h, AC_L_END)
            eng.wait_ge(s_pool.h, POOL_L_END)
            eng.wait_ge(s_sx1f.h, SX1)
            for k in range(8):
                eng.wait_ge(s_sp8[k].h, 32)

        esfa = es.enter_context(contextlib.ExitStack())
        sba2 = lambda name, shape, dt: esfa.enter_context(nc.sbuf_tensor(name, shape, dt))
        a_sb = sba2("a_sb", [128, 8 * 2048], BF16)
        g_sb = sba2("g_sb", [128, 8 * 64], BF16)
        e_sb = sba2("e_sb", [64, 8 * 128], BF16)
        hsum_sb = sba2("hsum_sb", [128, 16], BF16)
        hbc_sb = sba2("hbc_sb", [16, 128], BF16)
        neginf = sba2("neginf", [128, 64], F32)
        vecs_bf = sba2("vecs_bf", [128, 8 * 512], BF16)
        sq_sb = sba2("sq_sb", [128, 8 * 512], BF16)
        arr4 = sba2("arr4", [128, 256], F32)
        awork = [[sba2(f"awork{g}_{i}", [128, 64], F32) for i in range(2)] for g in range(4)]
        amsk = [sba2(f"amsk{g}", [128, 64], mybir.dt.uint8) for g in range(4)]
        amx = [sba2(f"amx{g}", [128, 1], F32) for g in range(4)]
        routes_tok = sba2("routes_tok", [128, 4 * 64], BF16)
        routesT_bf = sba2("routesT_bf", [64, 512], BF16)
        rbc_sb = sba2("rbc_sb", [128, 8 * 512], BF16)
        u1_t = [sba2(f"u1_t{i}", [128, 512], BF16) for i in range(2)]
        v_bf = sba2("v_bf", [16, 512], BF16)

        psV = [esfa.enter_context(nc.psum_tensor(f"psV{i}", [128, 512], F32)) for i in range(2)]
        psArr = esfa.enter_context(nc.psum_tensor("psArr", [128, 256], F32))
        psRt = esfa.enter_context(nc.psum_tensor("psRt", [64, 512], F32))
        psRbc = [esfa.enter_context(nc.psum_tensor(f"psRbc{i}", [128, 512], F32))
                 for i in range(2)]
        psV16 = esfa.enter_context(nc.psum_tensor("psV16", [16, 512], F32))
        psVbc = esfa.enter_context(nc.psum_tensor("psVbc", [128, 512], F32))

        def _sbrange(h):
            nb = 1
            for d in h.shape[1:]:
                nb *= int(d)
            nb *= mybir.dt.size(h.dtype)
            a0 = nc.lookup_mloc(h.name).addr
            return (a0, a0 + nb)

        def _max_y1h(hs):
            mx = -1
            for h in hs:
                r0, r1 = _sbrange(h)
                for fc in range(16):
                    z0, z1 = _sbrange(y1h[fc])
                    if r0 < z1 and z0 < r1:
                        mx = max(mx, fc)
            return mx

        def spill_wait(eng, mx, xc_edge=True):
            if mx < 0:
                return
            mx += 1 - mx % 2
            for k in range(mx // 2 + 1):
                eng.wait_ge(s_sp8[k].h, 32)
            eng.wait_ge(s_pool.h, POOL_MEMG + 8 + (mx + 1) // 2)
            if xc_edge:
                eng.wait_ge(s_dve.h, dvL(1, ("xc", mx - 1)))

        MA_A = _max_y1h([a_sb])
        MA_G = _max_y1h([g_sb])
        MA_VEC = _max_y1h([vecs_bf])
        MA_SQ = _max_y1h([sq_sb])
        MA_CON = _max_y1h([e_sb, hsum_sb, hbc_sb, neginf])
        MA_LATE = _max_y1h([arr4, routes_tok, routesT_bf, rbc_sb, v_bf]
                           + [w for g4 in awork for w in g4] + amsk + amx + u1_t)
        if globals().get('DBG_ADDR'):
            print("MA_A", MA_A, "MA_G", MA_G, "MA_VEC", MA_VEC, "MA_SQ", MA_SQ,
                  "MA_CON", MA_CON, "MA_LATE", MA_LATE)

        @blk.gpsimd
        def _(gpsimd):
            def a_load():
                spill_wait(gpsimd, MA_A)
                gpsimd.dma_start(
                    out=a_sb[:, :].rearrange("p (a c) -> p a c", c=2048),
                    in_=a_h[:, :, :].rearrange("a p c -> p a c")).then_inc(
                    s_fa.h, 16)

            def g_load():
                spill_wait(gpsimd, MA_G)
                gpsimd.dma_start(
                    out=g_sb[:, :].rearrange("p (a c) -> p a c", c=64),
                    in_=g_h[:, :, :].rearrange("a p c -> p a c")).then_inc(
                    s_fg.h, 16)

            gpsimd.wait_ge(s_pe.h, PE_L_END)
            if MA_A < 0:
                a_load()
            if MA_G < 0:
                g_load()
            for fc in range(1, 16, 2):
                gpsimd.wait_ge(s_dve.h, dvL(1, ("xna", fc)))
                p = gpsimd.tensor_copy(
                    out=x1bf[:, fc * TPC + 1024: fc * TPC + 2048],
                    in_=y1h[fc][:, :])
                p.then_inc(s_pool.h, 1)
                if MA_A >= 0 and fc in (MA_A, MA_A + 1):
                    a_load()
                if MA_G >= 0 and fc in (MA_G, MA_G + 1):
                    g_load()

        @blk.sync
        def _(sync):
            sync.wait_ge(s_pe.h, PE_L_END)
            spill_wait(sync, MA_CON)
            sync.dma_start(out=e_sb[:, :].rearrange("q (a c) -> q a c", c=128),
                           in_=e_h[:, :, :].rearrange("a q c -> q a c")).then_inc(s_fcl.h, 16)
            sync.dma_start(out=hsum_sb[:, :], in_=hsum_h[:, :]).then_inc(s_fcl.h, 16)
            sync.dma_start(out=hbc_sb[:, :], in_=hbc_h[:, :]).then_inc(s_fcl.h, 16)
            sync.dma_start(out=neginf[:, :], in_=neginf_h[:, :]).then_inc(s_fcl.h, 16)
            sync.dma_start(out=bias_sb[:, :], in_=bias_h[:, :]).then_inc(s_fcl.h, 16)


        @blk.tensor
        def _(tensor):
            tensor.wait_ge(s_dve.h, dvL(1, ("nmr", 1)))
            tensor.wait_ge(s_act.h, AC_L_END)
            tensor.wait_ge(s_pool.h, POOL_MEMG + 8)
            tensor.wait_ge(s_fa.h, 16)

            def emit_vec(s, acs=range(8)):
                if s == 2 and acs[0] == 0:
                    tensor.wait_ge(s_dve.h, DV_L_END)
                    tensor.wait_ge(s_pool.h, POOL_L_END)
                for ac in acs:
                    if ac >= 2:
                        tensor.wait_ge(s_act.h, acF(s, ("sqv", ac - 2)))
                        tensor.wait_ge(s_dve.h, dvF(s, ("vce", ac - 2)))
                    elif s > 0:
                        tensor.wait_ge(s_act.h, acF(s - 1, ("sqv", 6 + ac)))
                        tensor.wait_ge(s_dve.h, dvF(s - 1, ("vce", 6 + ac)))
                    for dc in range(16):
                        mm = tensor.matmul(
                            psV[ac % 2][:, :],
                            a_sb[:, ac * 2048 + dc * 128: ac * 2048 + (dc + 1) * 128],
                            x1bf[:, dc * TPC + s * SW: dc * TPC + (s + 1) * SW],
                            start=(dc == 0), stop=(dc == 15))
                    s_pe.inc(mm)

            emit_vec(0)
            for s in range(NS):
                if s == 0:
                    tensor.wait_ge(s_fg.h, 16)
                if s > 0:
                    tensor.wait_ge(s_act.h, acF(s - 1, ("arr_ev", 0)))
                for g4 in range(4):
                    tensor.wait_ge(s_act.h, acF(s, ("sqv", 7)))
                    for ac in range(8):
                        mm = tensor.matmul(
                            psArr[:, g4 * 64:(g4 + 1) * 64],
                            sq_sb[:, ac * 512 + g4 * 128: ac * 512 + (g4 + 1) * 128],
                            g_sb[:, ac * 64:(ac + 1) * 64],
                            start=(ac == 0), stop=(ac == 7))
                    s_pe.inc(mm)
                for g4 in range(4):
                    tensor.wait_ge(s_dve.h, dvF(s, ("rts", g4)))
                    if s > 0 and g4 == 0:
                        tensor.wait_ge(s_dve.h, dvF(s - 1, ("rtev", 0)))
                    mm = tensor.matmul(psRt[:, g4 * 128:(g4 + 1) * 128],
                                       routes_tok[:, g4 * 64:(g4 + 1) * 64],
                                       ident[:, :], start=True, stop=True)
                    s_pe.inc(mm)
                if s + 1 < NS:
                    emit_vec(s + 1, range(0, 2))
                if s == 0:
                    tensor.wait_ge(s_fcl.h, 16 * 5)
                for ac in range(8):
                    tensor.wait_ge(s_dve.h, dvF(s, ("rtev", 0)))
                    if s * 8 + ac >= 2:
                        pac, ps_ = (ac - 2, s) if ac >= 2 else (6 + ac, s - 1)
                        tensor.wait_ge(s_dve.h, dvF(ps_, ("u1", pac)))
                        tensor.wait_ge(s_act.h, acF(ps_, ("rbce", pac)))
                    mm = tensor.matmul(psRbc[ac % 2][:, :],
                                       e_sb[:, ac * 128:(ac + 1) * 128],
                                       routesT_bf[:, :], start=True, stop=True)
                    s_pe.inc(mm)
                    tensor.wait_ge(s_dve.h, dvF(s, ("u1", ac)))
                    mm = tensor.matmul(psV16[:, :], hsum_sb[:, :], u1_t[ac % 2][:, :],
                                       start=(ac == 0), stop=(ac == 7))
                    s_pe.inc(mm)
                tensor.wait_ge(s_act.h, acF(s, ("v16ev", 0)))
                if s > 0:
                    tensor.wait_ge(s_dve.h, dvF(s - 1, ("u2", 7)))
                mm = tensor.matmul(psVbc[:, :], hbc_sb[:, :], v_bf[:, :],
                                   start=True, stop=True)
                s_pe.inc(mm)
                if s + 1 < NS:
                    emit_vec(s + 1, range(2, 8))

        @blk.vector
        def _(vector):
            vector.wait_ge(s_pe.h, PE_L_END)
            spill_wait(vector, MA_VEC, xc_edge=False)
            for s in range(NS):
                for ac in range(8):
                    vector.wait_ge(s_pe.h, peF(s, ("vec", ac)))
                    if s > 0 and ac < 2:
                        vector.wait_ge(s_dve.h, dvF(s - 1, ("u1", ac)))
                    v = vector.tensor_copy(out=vecs_bf[:, ac * SW:(ac + 1) * SW],
                                           in_=psV[ac % 2][:, :])
                    s_dve.inc(v)
                if s == 0:
                    vector.wait_ge(s_fcl.h, 16 * 5)
                vector.wait_ge(s_act.h, acF(s, ("arr_ev", 0)))
                if s > 0:
                    for g4 in range(4):
                        vector.wait_ge(s_dve.h, dvF(s - 1, ("rts", g4)))
                a0s = [arr4[:, g4 * 64:(g4 + 1) * 64] for g4 in range(4)]
                for step in range(11):
                    for g4 in range(4):
                        a0 = a0s[g4]
                        if step > 0:
                            vector.wait_ge(s_dve.h, dvF(s, (f"t4_{step - 1}", g4)))
                        if step == 0:
                            v = vector.reduce_max(out=amx[g4][:, :], in_=a0, axis=AX.X)
                        elif step in (1, 4, 7):
                            srcs = {1: a0, 4: awork[g4][0][:, :], 7: awork[g4][1][:, :]}[step]
                            v = vector.tensor_scalar(out=amsk[g4][:, :], in0=srcs,
                                                     scalar1=amx[g4][:, :], scalar2=None,
                                                     op0=ALU.is_ge)
                        elif step == 2:
                            v = vector.select(out=awork[g4][0][:, :], mask=amsk[g4][:, :],
                                              on_true=neginf[:, :], on_false=a0,
                                              add_drain=True)
                        elif step == 5:
                            v = vector.select(out=awork[g4][1][:, :], mask=amsk[g4][:, :],
                                              on_true=neginf[:, :],
                                              on_false=awork[g4][0][:, :], add_drain=True)
                        elif step == 8:
                            v = vector.select(out=awork[g4][0][:, :], mask=amsk[g4][:, :],
                                              on_true=neginf[:, :],
                                              on_false=awork[g4][1][:, :], add_drain=True)
                        elif step in (3, 6):
                            srcr = awork[g4][0][:, :] if step == 3 else awork[g4][1][:, :]
                            v = vector.reduce_max(out=amx[g4][:, :], in_=srcr, axis=AX.X)
                        elif step == 9:
                            v = vector.reduce_max(out=amx[g4][:, :], in_=awork[g4][0][:, :],
                                                  axis=AX.X)
                        else:
                            if s > 0:
                                vector.wait_ge(s_pe.h, peF(s - 1, ("rt", g4)))
                            v = vector.tensor_scalar(
                                out=routes_tok[:, g4 * 64:(g4 + 1) * 64],
                                in0=a0, scalar1=amx[g4][:, :], scalar2=0.25,
                                op0=ALU.is_ge, op1=ALU.mult)
                        s_dve.inc(v)
                vector.wait_ge(s_pe.h, peF(s, ("rt", 3)))
                v = vector.tensor_copy(out=routesT_bf[:, :], in_=psRt[:, :])
                s_dve.inc(v)
                for ac in range(8):
                    vector.wait_ge(s_pe.h, peF(s, ("rbc", ac)))
                    if s * 8 + ac >= 2:
                        pac, ps_ = (ac - 2, s) if ac >= 2 else (6 + ac, s - 1)
                        vector.wait_ge(s_pe.h, peF(ps_, ("vsum", pac)))
                    v = vector.tensor_tensor(
                        out=u1_t[ac % 2][:, :], in0=psRbc[ac % 2][:, :],
                        in1=vecs_bf[:, ac * SW:(ac + 1) * SW], op=ALU.mult)
                    s_dve.inc(v)
                for ac in range(8):
                    vector.wait_ge(s_pe.h, peF(s, ("vbc", 0)))
                    vector.wait_ge(s_act.h, acF(s, ("rbce", ac)))
                    v = vector.tensor_tensor(
                        out=u2f[:, ac * TPC + s * SW: ac * TPC + (s + 1) * SW],
                        in0=psVbc[:, :], in1=rbc_sb[:, ac * 512:(ac + 1) * 512],
                        op=ALU.mult)
                    s_dve.inc(v)

        @blk.scalar
        def _(scalar):
            scalar.wait_ge(s_pe.h, PE_L_END)
            spill_wait(scalar, MA_SQ)
            for s in range(NS):
                for ac in range(8):
                    scalar.wait_ge(s_pe.h, peF(s, ("vec", ac)))
                    scalar.wait_ge(s_dve.h, dvF(s, ("vce", ac)))
                    if s > 0 and ac == 0:
                        scalar.wait_ge(s_pe.h, peF(s - 1, ("arr", 3)))
                    a = scalar.activation(out=sq_sb[:, ac * 512:(ac + 1) * 512],
                                          in_=psV[ac % 2][:, :], func=AF.Square)
                    s_act.inc(a)
                if s == 0:
                    spill_wait(scalar, MA_LATE)
                scalar.wait_ge(s_pe.h, peF(s, ("arr", 3)))
                s_act.inc(scalar.copy(out=arr4[:, :], in_=psArr[:, :]))
                for ac in range(8):
                    scalar.wait_ge(s_pe.h, peF(s, ("rbc", ac)))
                    scalar.wait_ge(s_dve.h, dvF(s, ("u1", ac)))
                    a = scalar.copy(out=rbc_sb[:, ac * 512:(ac + 1) * 512],
                                    in_=psRbc[ac % 2][:, :])
                    s_act.inc(a)
                scalar.wait_ge(s_pe.h, peF(s, ("vsum", 7)))
                s_act.inc(scalar.copy(out=v_bf[:, :], in_=psV16[:, :]))

        esfa.close()
        esfe = es.enter_context(contextlib.ExitStack())
        sbe = lambda name, shape, dt: esfe.enter_context(nc.sbuf_tensor(name, shape, dt))
        wf_g = sbe("wf_g", [128, 8 * 2048], BF16)
        b2_g = sbe("b2_g", [128, 8 * 1024], BF16)
        _x1g = sbe("x1g0", [128, 8 * 512], F32)
        x1g = [_x1g, _x1g]
        y2t = [sbe(f"y2t{i}", [128, 512], F32) for i in range(2)]
        ybf2 = [sbe(f"ybf2{i}", [128, 512], BF16) for i in range(2)]
        sqbf2 = [sbe(f"sqbf2{i}", [128, 512], BF16) for i in range(2)]
        sum_acc = sbe("sum_acc", [1, TPC], F32)
        sqr_acc = sbe("sqr_acc", [1, TPC], F32)
        psE = [esfe.enter_context(nc.psum_tensor(f"psE{i}", [128, 512], F32)) for i in range(2)]
        psSm = esfe.enter_context(nc.psum_tensor("psSm", [1, 512], F32))
        psSq = esfe.enter_context(nc.psum_tensor("psSq", [1, 512], F32))

        def ebarrier(eng):
            eng.wait_ge(s_pe.h, PE_F_END)
            eng.wait_ge(s_dve.h, DV_F_END)
            eng.wait_ge(s_act.h, AC_F_END)
            eng.wait_ge(s_sx1f.h, SX1)
            for k in range(8):
                eng.wait_ge(s_sp8[k].h, 32)
            eng.wait_ge(s_pool.h, POOL_L_END)

        @blk.gpsimd
        def _(gpsimd):
            ebarrier(gpsimd)

            def wf_load(g, half):
                gpsimd.dma_start(
                    out=wf_g[:, half * 8192:(half + 1) * 8192].rearrange(
                        "p (j c) -> p j c", c=2048),
                    in_=wf_h[g * 8 + half * 4: g * 8 + half * 4 + 4, :, :].rearrange(
                        "j p c -> p j c")).then_inc(s_wfg[half].h, 16)

            def b2_load(g):
                for ac in range(8):
                    gpsimd.dma_start(
                        out=b2_g[:, ac * 1024:(ac + 1) * 1024],
                        in_=b2_h[ac, :, g * 1024:(g + 1) * 1024]).then_inc(s_b2g.h, 16)

            wf_load(0, 0)
            wf_load(0, 1)
            b2_load(0)
            for g in range(2):
                for s in range(NS):
                    for jci in range(8):
                        jc = g * 8 + jci
                        gpsimd.wait_ge(s_dve.h, dvE(g, s, ("ye2", jci)))
                        gpsimd.dma_start(out=sp_y2[jc, :, s * SW:(s + 1) * SW],
                                         in_=y2t[jci % 2][:, :]).then_inc(
                            s_y2s[g][jci % 2].h, 16)
                        if g == 0 and s == NS - 1 and jci == 3:
                            wf_load(1, 0)
                        if g == 0 and s == NS - 1 and jci == 7:
                            wf_load(1, 1)
                            b2_load(1)

        @blk.sync
        def _(sync):
            ebarrier(sync)
            for g in range(2):
                for s in range(NS):
                    kk = g * NS + s
                    if kk >= 1:
                        pg, ps_ = divmod(kk - 1, NS)
                        sync.wait_ge(s_dve.h, dvE(pg, ps_, ("ye2", 7)))
                    sync.dma_start(
                        out=x1g[s % 2][:, :].rearrange("p (j t) -> p j t", t=512),
                        in_=sp_x1[g * 8:(g + 1) * 8, :, s * SW:(s + 1) * SW].rearrange(
                            "j p t -> p j t")).then_inc(s_x1g[0].h, 16)

        @blk.tensor
        def _(tensor):
            for g in range(2):
                tensor.wait_ge(s_wfg[0].h, 16 * (g + 1))
                for s in range(NS):
                    def emit_estats(jci):
                        tensor.wait_ge(s_dve.h, dvE(g, s, ("yc2", jci)))
                        if jci == 0 and (g > 0 or s > 0):
                            tensor.wait_ge(s_dve.h, dvE(*divmod(g * NS + s - 1, NS), ("sadd",)))
                        mm = tensor.matmul(psSm[:, :], onesc[:, :], ybf2[jci % 2][:, :],
                                           start=(jci == 0), stop=(jci == 7))
                        s_pe.inc(mm)
                        tensor.wait_ge(s_act.h, acE(g, s, jci))
                        if jci == 0 and (g > 0 or s > 0):
                            tensor.wait_ge(s_dve.h, dvE(*divmod(g * NS + s - 1, NS), ("qadd",)))
                        mm = tensor.matmul(psSq[:, :], onesc[:, :], sqbf2[jci % 2][:, :],
                                           start=(jci == 0), stop=(jci == 7))
                        s_pe.inc(mm)

                    for jci in range(8):
                        if s == 0 and jci == 4:
                            tensor.wait_ge(s_wfg[1].h, 16 * (g + 1))
                        kk = (g * NS + s) * 8 + jci
                        if kk >= 2:
                            pg, prem = divmod(kk - 2, NS * 8)
                            ps_, pjci = divmod(prem, 8)
                            tensor.wait_ge(s_dve.h, dvE(pg, ps_, ("ye2", pjci)))
                        for dc in range(16):
                            tensor.matmul(
                                psE[jci % 2][:, :],
                                wf_g[:, jci * 2048 + dc * 128: jci * 2048 + (dc + 1) * 128],
                                x1bf[:, dc * TPC + s * SW: dc * TPC + (s + 1) * SW],
                                start=(dc == 0), stop=False)
                        if s == 0 and jci == 0:
                            tensor.wait_ge(s_b2g.h, 16 * 8 * (g + 1))
                        for ac in range(8):
                            mm = tensor.matmul(
                                psE[jci % 2][:, :],
                                b2_g[:, ac * 1024 + jci * 128: ac * 1024 + (jci + 1) * 128],
                                u2f[:, ac * TPC + s * SW: ac * TPC + (s + 1) * SW],
                                start=False, stop=(ac == 7))
                        s_pe.inc(mm)
                        if jci >= 1:
                            emit_estats(jci - 1)
                    emit_estats(7)

        @blk.vector
        def _(vector):
            ebarrier(vector)
            vector.memset(sum_acc[:, :], 0.0)
            vector.memset(sqr_acc[:, :], 0.0)
            for g in range(2):
                for s in range(NS):
                    for jci in range(8):
                        jc = g * 8 + jci
                        vector.wait_ge(s_pe.h, peE(g, s, jci, "emm"))
                        vector.wait_ge(s_x1g[0].h, 16 * (g * NS + s + 1))
                        if (g * NS + s) * 8 + jci >= 2:
                            kkp = (g * NS + s) * 8 + jci - 2
                            pg2 = kkp // (NS * 8)
                            within = kkp % (NS * 8)
                            vector.wait_ge(s_y2s[pg2][jci % 2].h, 16 * (within // 2 + 1))
                        v = vector.scalar_tensor_tensor(
                            out=y2t[jci % 2][:, :], in0=psE[jci % 2][:, :],
                            scalar=bias_sb[:, jc:jc + 1],
                            in1=x1g[s % 2][:, jci * 512:(jci + 1) * 512],
                            op0=ALU.add, op1=ALU.add)
                        s_dve.inc(v)
                        vector.wait_ge(s_dve.h, dvE(g, s, ("ye2", jci)))
                        if (g * NS + s) * 8 + jci >= 2:
                            pg, prem = divmod((g * NS + s) * 8 + jci - 2, NS * 8)
                            ps_, pjci = divmod(prem, 8)
                            vector.wait_ge(s_pe.h, peE(pg, ps_, pjci, "es"))
                        v = vector.tensor_copy(out=ybf2[jci % 2][:, :],
                                               in_=y2t[jci % 2][:, :])
                        s_dve.inc(v)
                    vector.wait_ge(s_pe.h, peE(g, s, 7, "es"))
                    v = vector.scalar_tensor_tensor(
                        out=sum_acc[:, s * SW:(s + 1) * SW], in0=psSm[:, :], scalar=0.0,
                        in1=sum_acc[:, s * SW:(s + 1) * SW], op0=ALU.bypass, op1=ALU.add)
                    s_dve.inc(v)
                    vector.wait_ge(s_pe.h, peE(g, s, 7, "eq"))
                    v = vector.scalar_tensor_tensor(
                        out=sqr_acc[:, s * SW:(s + 1) * SW], in0=psSq[:, :], scalar=0.0,
                        in1=sqr_acc[:, s * SW:(s + 1) * SW], op0=ALU.bypass, op1=ALU.add)
                    s_dve.inc(v)

        @blk.scalar
        def _(scalar):
            ebarrier(scalar)
            for g in range(2):
                for s in range(NS):
                    for jci in range(8):
                        scalar.wait_ge(s_dve.h, dvE(g, s, ("ye2", jci)))
                        if (g * NS + s) * 8 + jci >= 2:
                            pg, prem = divmod((g * NS + s) * 8 + jci - 2, NS * 8)
                            ps_, pjci = divmod(prem, 8)
                            scalar.wait_ge(s_pe.h, peE(pg, ps_, pjci, "eq"))
                        a = scalar.activation(out=sqbf2[jci % 2][:, :],
                                              in_=y2t[jci % 2][:, :], func=AF.Square)
                        s_act.inc(a)

        esfe.close()
        esn = es.enter_context(contextlib.ExitStack())
        sbn = lambda name, shape, dt: esn.enter_context(nc.sbuf_tensor(name, shape, dt))
        nm2 = sbn("nm2", [1, TPC], F32)
        msq2 = sbn("msq2", [1, TPC], F32)
        var2 = sbn("var2", [1, TPC], F32)
        rstd2 = sbn("rstd2", [1, TPC], F32)
        nmr2 = sbn("nmr2", [1, TPC], F32)
        rstd2_bc = sbn("rstd2_bc", [128, TPC], F32)
        nmr2_bc = sbn("nmr2_bc", [128, TPC], F32)
        psBc2 = esn.enter_context(nc.psum_tensor("psBc2", [128, TPC], F32))

        @blk.vector
        def _(vector):
            vector.wait_ge(s_dve.h, DV_E_END)
            s_dve.inc(vector.tensor_scalar_mul(out=nm2[:, :], in0=sum_acc[:, :],
                                               scalar1=-1.0 / D))
            s_dve.inc(vector.tensor_scalar_mul(out=msq2[:, :], in0=sqr_acc[:, :],
                                               scalar1=1.0 / D))
            vector.wait_ge(s_dve.h, dvN("nm"))
            s_dve.inc(vector.tensor_mul(out=var2[:, :], in0=nm2[:, :], in1=nm2[:, :]))
            vector.wait_ge(s_dve.h, dvN("mm2"))
            vector.wait_ge(s_dve.h, dvN("msq"))
            s_dve.inc(vector.tensor_sub(out=var2[:, :], in0=msq2[:, :], in1=var2[:, :]))
            vector.wait_ge(s_act.h, acN("rstd"))
            s_dve.inc(vector.tensor_mul(out=nmr2[:, :], in0=nm2[:, :], in1=rstd2[:, :]))
            for jc in range(0, 16, 2):
                vector.wait_ge(s_act.h, acN("nbc_ev"))
                vector.wait_ge(s_y2l[0].h, 16 * (jc // 2 + 1))
                v = vector.tensor_mul(out=y2l[0][:, :], in0=y2l[0][:, :],
                                      in1=rstd2_bc[:, :])
                s_dve.inc(v)
                vector.wait_ge(s_dve.h, dvN(f"xm{jc}"))
                v = vector.tensor_add(out=y2l[0][:, :], in0=y2l[0][:, :],
                                      in1=nmr2_bc[:, :])
                s_dve.inc(v)

        @blk.gpsimd
        def _(gpsimd):
            gpsimd.wait_ge(s_pool.h, POOL_MEMG)
            gpsimd.wait_ge(s_act.h, acN("nbc_ev"))
            for jc in range(16):
                if jc % 2 == 1:
                    gpsimd.wait_ge(s_y2l[1].h, 16 * (jc // 2 + 1))
                    p = gpsimd.tensor_tensor(out=y2l[1][:, :], in0=y2l[1][:, :],
                                             in1=rstd2_bc[:, :], op=ALU.mult)
                    p.then_inc(s_pool.h, 1)
                    gpsimd.wait_ge(s_pool.h, plN(f"xm{jc}"))
                    p = gpsimd.tensor_tensor(out=y2l[1][:, :], in0=y2l[1][:, :],
                                             in1=nmr2_bc[:, :], op=ALU.add)
                    p.then_inc(s_pool.h, 1)
                    gpsimd.wait_ge(s_pool.h, plN(f"xa{jc}"))
                else:
                    gpsimd.wait_ge(s_dve.h, dvN(f"xa{jc}"))
                gpsimd.dma_start(out=out_h[jc * 128:(jc + 1) * 128, :],
                                 in_=y2l[jc % 2][:, :]).then_inc(s_o2[jc % 2].h, 16)
        s_pool.n = POOL_L_END + 16

        @blk.scalar
        def _(scalar):
            scalar.wait_ge(s_act.h, AC_E_END)
            scalar.wait_ge(s_dve.h, dvN("var"))
            a = scalar.activation(out=var2[:, :], in_=var2[:, :], func=AF.Ln,
                                  bias=eps_sb[0:1, 0:1])
            s_act.inc(a)
            scalar.wait_ge(s_act.h, acN("lv"))
            a = scalar.activation(out=rstd2[:, :], in_=var2[:, :], func=AF.Exp, scale=-0.5)
            s_act.inc(a)
            scalar.wait_ge(s_pe.h, peN(3))
            s_act.inc(scalar.copy(out=rstd2_bc[:, :], in_=psBc2[:, :]))
            scalar.wait_ge(s_pe.h, peN(7))
            s_act.inc(scalar.copy(out=nmr2_bc[:, :], in_=psBc2[:, :]))

        @blk.tensor
        def _(tensor):
            tensor.wait_ge(s_act.h, acN("rstd"))
            for s in range(NS):
                mm = tensor.matmul(psBc2[:, s * SW:(s + 1) * SW], ones128[0:1, :],
                                   rstd2[:, s * SW:(s + 1) * SW], start=True, stop=True)
                s_pe.inc(mm)
            tensor.wait_ge(s_act.h, acN("rbc_ev"))
            tensor.wait_ge(s_dve.h, dvN("nmr"))
            for s in range(NS):
                mm = tensor.matmul(psBc2[:, s * SW:(s + 1) * SW], ones128[0:1, :],
                                   nmr2[:, s * SW:(s + 1) * SW], start=True, stop=True)
                s_pe.inc(mm)

        @blk.sync
        def _(sync):
            for jc in range(16):
                par = jc % 2
                sync.wait_ge(s_y2s[jc // 8][par].h, 16 * 16)
                if jc >= 2:
                    key = f"xa{jc - 2}"
                    if (jc - 2) % 2 == 0:
                        sync.wait_ge(s_dve.h, dvN(key))
                    else:
                        sync.wait_ge(s_pool.h, plN(key))
                    sync.wait_ge(s_o2[jc % 2].h, 16 * (jc // 2))
                sync.dma_start(out=y2l[par][:, :],
                               in_=sp_y2[jc, :, :]).then_inc(s_y2l[par].h, 16)

        @blk.sync
        def _(sync):
            sync.wait_ge(s_o2[0].h, 16 * 8)
            sync.wait_ge(s_o2[1].h, 16 * 8)

    return nc


def kernel(**inputs):
    in_maps = prep_inputs(inputs)
    nc = build(dbg=False, upto="all")
    res = run_bass_kernel_spmd(nc, in_maps, core_ids=list(range(NCORES)))
    out = np.zeros((B, S, D), np.float32)
    for c in range(NCORES):
        b, q4 = c // 4, c % 4
        out[b, q4 * TPC:(q4 + 1) * TPC, :] = res.results[c]["out"].T
    return out
